# version 02 (final)
# baseline (speedup 1.0000x reference)
import numpy as np
import ml_dtypes
from contextlib import ExitStack

import concourse.bass as bass
import concourse.mybir as mybir
from concourse.bass_utils import run_bass_kernel_spmd
from concourse.library_overlay import lower_extended_insts

F32 = mybir.dt.float32
BF16 = mybir.dt.bfloat16
I32 = mybir.dt.int32
AF = mybir.ActivationFunctionType
ALU = mybir.AluOpType
AX = mybir.AxisListType

BF = ml_dtypes.bfloat16

B, S, D = 2, 4096, 4096
NCORE = 8
TLOC = 1024
NCH = D // 128
QSCALE = 1.0 / np.sqrt(128.0)
LN_EPS = 1e-5
ALIBI_SLOPES = [2.0 ** (-(h + 1) / 4.0) for h in range(32)]

ENGINES = ("sync", "gpsimd", "scalar", "vector", "tensor")


class Sched:

    def __init__(self):
        self.ops = []
        self.cnt = {}

    def c(self, name):
        return self.cnt.get(name, 0)

    def add(self, engine, emit, waits=(), inc=None):
        ws = []
        for w in waits:
            if isinstance(w, str):
                v = self.cnt.get(w, 0)
                ws.append((w, v))
            else:
                ws.append((w[0], w[1]))
        incs = []
        if inc is not None:
            inc_list = inc if isinstance(inc, list) else [inc]
            for it in inc_list:
                if it is None:
                    continue
                n, k = it
                self.cnt[n] = self.cnt.get(n, 0) + k
                incs.append((n, k))
        self.ops.append((engine, emit, ws, incs))

    def full_fence(self):
        snap = [(n, v) for n, v in sorted(self.cnt.items()) if v > 0]
        for eng in ENGINES:
            self.add(eng, lambda e: e.nop(), waits=snap)

    def barrier(self):
        name = "bar"
        base = self.c(name)
        for eng in ENGINES:
            self.add(eng, lambda e: e.nop(), inc=(name, 1))
        tgt = base + len(ENGINES)
        for eng in ENGINES:
            self.add(eng, lambda e: e.nop(), waits=[(name, tgt)])

    def realize(self, nc, es, block):
        sems = {n: es.enter_context(nc.semaphore(n)) for n in self.cnt}
        per = {eng: [] for eng in ENGINES}
        for op in self.ops:
            per[op[0]].append(op)

        def emit_engine(e, eng):
            last_wait = {}
            for (_, emit, ws, incs) in per[eng]:
                for (n, v) in ws:
                    if v <= 0 or last_wait.get(n, -1) >= v or n not in sems:
                        continue
                    e.wait_ge(sems[n], v)
                    last_wait[n] = v
                instr = emit(e)
                for (n, k) in incs:
                    assert instr is not None, f"inc {n} on op returning None"
                    instr.then_inc(sems[n], k)

        for eng in ENGINES:
            if not per[eng]:
                continue
            getattr(block, eng)(lambda e, _eng=eng: emit_engine(e, _eng))


def _sb(nc, es):
    return lambda name, shape, dt=F32: es.enter_context(nc.sbuf_tensor(name, shape, dt))


def _ps(nc, es):
    return lambda name, shape, dt=F32: es.enter_context(nc.psum_tensor(name, shape, dt))



def emit_ln(s, bufs, xsrc_ap, act_ap, tag, ntok=TLOC):
    xs, xb, sq = bufs["xs"], bufs["xb"], bufs["sq"]
    ones128, epsc = bufs["ones128"], bufs["epsc"]
    rt, mrt, scr, aps = bufs["rt"], bufs["mrt"], bufs["scr"], bufs["aps"]
    pSx, pSxx = bufs["pSx"], bufs["pSxx"]
    ld, ca, sqd, smm, sfin, apd = (tag + n for n in ("ld", "ca", "sqd", "smm", "sfin", "ap"))
    ntb = ntok // 512
    b_smm, b_apd = s.c(smm), s.c(apd)

    for c in range(NCH):
        w = [("init", 2)]
        if c >= 2:
            w += [(ca, s.c(ca) - 1), (sqd, s.c(sqd) - 1)]
        s.add("sync", lambda e, c=c: e.dma_start(out=xs[c % 2][:], in_=xsrc_ap[128 * c:128 * (c + 1), :]),
              waits=w, inc=(ld, 16))
        s.add("gpsimd", lambda e, c=c: e.tensor_copy(xb[c % 2][:], xs[c % 2][:]),
              waits=[(ld, s.c(ld))], inc=(ca, 1))
        s.add("scalar", lambda e, c=c: e.activation(sq[c % 2][:], xs[c % 2][:], AF.Square),
              waits=[(ld, s.c(ld))], inc=(sqd, 1))
        for t in range(ntb):
            cs = slice(512 * t, 512 * (t + 1))
            s.add("tensor", lambda e, c=c, cs=cs, t=t: e.matmul(
                pSx[t][:], ones128[:], xb[c % 2][:, cs], start=(c == 0), stop=(c == NCH - 1)),
                waits=[("init", 2), (ca, s.c(ca))], inc=(smm, 1))
            s.add("tensor", lambda e, c=c, cs=cs, t=t: e.matmul(
                pSxx[t][:], ones128[:], sq[c % 2][:, cs], start=(c == 0), stop=(c == NCH - 1)),
                waits=[(sqd, s.c(sqd))], inc=(smm, 1))
    for t in range(ntb):
        s.add("vector", lambda e, t=t: e.tensor_scalar_mul(scr[t][:], pSx[t][:], 1.0 / D),
              waits=[(smm, s.c(smm))])
        s.add("vector", lambda e, t=t: e.tensor_scalar_mul(scr[2 + t][:], pSxx[t][:], 1.0 / D))
        s.add("vector", lambda e, t=t: e.tensor_mul(rt[t][:], scr[t][:], scr[t][:]))
        s.add("vector", lambda e, t=t: e.tensor_sub(scr[2 + t][:], scr[2 + t][:], rt[t][:]),
              inc=(sfin, 1))
        s.add("scalar", lambda e, t=t: e.activation(scr[2 + t][:], scr[2 + t][:], AF.Sqrt, bias=epsc[:]),
              waits=[("init", 2), (sfin, s.c(sfin))], inc=(sfin, 1))
        s.add("vector", lambda e, t=t: e.reciprocal(rt[t][:], scr[2 + t][:]),
              waits=[(sfin, s.c(sfin))])
        s.add("vector", lambda e, t=t: e.tensor_mul(mrt[t][:], scr[t][:], rt[t][:]),
              inc=(sfin, 1))
    sfin_full = s.c(sfin)
    for c in range(NCH):
        w = [(sfin, sfin_full)]
        if c >= 2:
            w.append((apd, b_apd + ntb * (c - 1)))
        s.add("sync", lambda e, c=c: e.dma_start(out=xs[c % 2][:], in_=xsrc_ap[128 * c:128 * (c + 1), :]),
              waits=w, inc=(ld, 16))
        for t in range(ntb):
            colp = slice(512 * t, 512 * (t + 1))
            cola = slice(ntok * c + 512 * t, ntok * c + 512 * (t + 1))
            s.add("vector", lambda e, c=c, t=t, colp=colp: e.tensor_mul(
                aps[t][:], xs[c % 2][:, colp], rt[t][:]),
                waits=[(ld, s.c(ld)), (sfin, sfin_full)])
            s.add("vector", lambda e, c=c, cola=cola, t=t: e.tensor_tensor(
                act_ap[:, cola], aps[t][:], mrt[t][:], op=ALU.subtract), inc=(apd, 1))
    return {"apd": apd, "apd_full": s.c(apd)}


def emit_proj_multi(s, bufs, specs, act_ap, tag, act_ready=None, first_wait=(), ntok=TLOC):
    wst, wbf, ost = bufs["wst"], bufs["wbf"], bufs["ost"]
    pP = bufs["pP"]
    lw, cw, pm, pe, so = (tag + n for n in ("lw", "cw", "pm", "pe", "so"))
    ntb = ntok // 512
    b_pm, b_pe, b_so = s.c(pm), s.c(pe), s.c(so)
    single_wst = wst[0] is wst[1]
    flat = [(sp, j) for sp in specs for j in range(sp["oc"])]
    for ow, (sp, j) in enumerate(flat):
        if ow < 2:
            w = list(first_wait)
            if single_wst and ow == 1:
                w.append((cw, s.c(cw)))
        else:
            w = [(cw, s.c(cw) if single_wst else s.c(cw) - 1)]
        s.add("sync", lambda e, ow=ow, sp=sp, j=j: e.dma_start(
            out=wst[ow % 2][:].rearrange("p (c o) -> p c o", c=NCH),
            in_=sp["w"][:, 128 * j:128 * (j + 1)].rearrange("(c p) o -> p c o", c=NCH)),
            waits=w, inc=(lw, 16))
        wc = [(lw, s.c(lw))]
        if ow >= 2:
            wc.append((pm, ntb * (ow - 1)))
        s.add("gpsimd", lambda e, ow=ow: e.tensor_copy(wbf[ow % 2][:], wst[ow % 2][:]),
              waits=wc, inc=(cw, 1))
        for tb in range(ntb):
            bank = (ow % 2) * ntb + tb
            for dc in range(NCH):
                wt = []
                if dc == 0:
                    wt = [(cw, s.c(cw))]
                    if act_ready:
                        wt.append(act_ready)
                    if ow >= 2:
                        wt.append((pe, ntb * (ow - 2) + tb + 1))
                last = (dc == NCH - 1)
                s.add("tensor", lambda e, ow=ow, tb=tb, dc=dc, last=last, bank=bank: e.matmul(
                    pP[bank][:], wbf[ow % 2][:, 128 * dc:128 * (dc + 1)],
                    act_ap[:, ntok * dc + 512 * tb: ntok * dc + 512 * (tb + 1)],
                    start=(dc == 0), stop=last),
                    waits=wt, inc=((pm, 1) if last else None))
            ev_w = [(pm, s.c(pm))]
            if ow >= 2 and tb == 0:
                ev_w.append((so, 16 * (ow - 1)))
            kw = {}
            if sp.get("bias") is not None:
                kw = {"bias": sp["bias"](j)}
            s.add("scalar", lambda e, ow=ow, tb=tb, sp=sp, kw=kw, bank=bank: e.activation(
                ost[ow % 2][:, 512 * tb:512 * (tb + 1)], pP[bank][:],
                AF.Copy, scale=sp["scale"], **kw),
                waits=ev_w, inc=(pe, 1))
        s.add("gpsimd", lambda e, ow=ow, sp=sp, j=j: e.dma_start(out=sp["dst"](j), in_=ost[ow % 2][:]),
              waits=[(pe, s.c(pe))], inc=(so, 16))
    return {"lw": lw, "cw": cw, "pm": pm, "pe": pe, "so": so,
            "pm_full": s.c(pm), "pe_full": s.c(pe), "so_full": s.c(so)}



def build_launchA():
    nc = bass.Bass("TRN2", target_bir_lowering=False)
    xT = nc.dram_tensor("xT", [D, TLOC], F32, kind="ExternalInput").ap()
    Wq = nc.dram_tensor("Wq", [D, 4096], F32, kind="ExternalInput").ap()
    Wk = nc.dram_tensor("Wk", [D, 512], F32, kind="ExternalInput").ap()
    Wv = nc.dram_tensor("Wv", [D, 512], F32, kind="ExternalInput").ap()
    QT = nc.dram_tensor("QT", [4096, TLOC], BF16, kind="ExternalOutput").ap()
    KT = nc.dram_tensor("KT", [512, TLOC], BF16, kind="ExternalOutput").ap()
    VL = nc.dram_tensor("VL", [TLOC, 512], BF16, kind="ExternalOutput").ap()

    es = ExitStack()
    sb, ps = _sb(nc, es), _ps(nc, es)
    bufs = {
        "xs": [sb(f"xs{i}", [128, TLOC]) for i in range(2)],
        "xb": [sb(f"xb{i}", [128, TLOC], BF16) for i in range(2)],
        "sq": [sb(f"sq{i}", [128, TLOC], BF16) for i in range(2)],
        "ones128": sb("ones128", [128, 128], BF16),
        "epsc": sb("epsc", [128, 1]),
        "rt": [sb(f"rt{t}", [128, 512]) for t in range(2)],
        "mrt": [sb(f"mrt{t}", [128, 512]) for t in range(2)],
        "scr": [sb(f"scr{t}", [128, 512]) for t in range(4)],
        "aps": [sb(f"aps{t}", [128, 512]) for t in range(2)],
        "wst": [sb(f"wst{i}", [128, 4096]) for i in range(2)],
        "wbf": [sb(f"wbf{i}", [128, 4096], BF16) for i in range(2)],
        "ost": [sb(f"ost{i}", [128, TLOC], BF16) for i in range(2)],
    }
    ACT = sb("ACT", [128, NCH * TLOC], BF16)
    pbanks = [ps(f"pb{i}", [128, 512]) for i in range(8)]
    bufs["pSx"] = pbanks[0:2]
    bufs["pSxx"] = pbanks[2:4]
    bufs["pP"] = pbanks[4:8]
    wvst = [sb(f"wvst{i}", [128, 512]) for i in range(2)]
    wvbf = [sb(f"wvbf{i}", [128, 512], BF16) for i in range(2)]
    vst = [sb(f"vst{i}", [128, 512], BF16) for i in range(2)]

    s = Sched()
    block = es.enter_context(nc.Block())
    s.add("gpsimd", lambda e: e.memset(bufs["epsc"][:], LN_EPS), inc=("init", 1))
    s.add("gpsimd", lambda e: e.memset(bufs["ones128"][:], 1.0), inc=("init", 1))

    ln = emit_ln(s, bufs, xT, ACT, tag="a")
    act_ready = (ln["apd"], ln["apd_full"])

    specs = [
        {"w": Wq, "oc": 32, "scale": QSCALE, "dst": lambda j: QT[128 * j:128 * (j + 1), :]},
        {"w": Wk, "oc": 4, "scale": 1.0, "dst": lambda j: KT[128 * j:128 * (j + 1), :]},
    ]
    pj = emit_proj_multi(s, bufs, specs, ACT, tag="q", act_ready=act_ready)

    vfirst = [(pj["pe"], pj["pe_full"]), ("asfin", s.c("asfin"))]
    for dc in range(NCH):
        w = list(vfirst) if dc < 2 else [("vcw", s.c("vcw") - 1)]
        s.add("sync", lambda e, dc=dc: e.dma_start(out=wvst[dc % 2][:], in_=Wv[128 * dc:128 * (dc + 1), :]),
              waits=w, inc=("vlw", 16))
        wc = [("vlw", s.c("vlw"))]
        if dc >= 2:
            wc.append(("vpm", 8 * (dc - 1)))
        s.add("gpsimd", lambda e, dc=dc: e.tensor_copy(wvbf[dc % 2][:], wvst[dc % 2][:]),
              waits=wc, inc=("vcw", 1))
        for tc in range(8):
            wt = [("vcw", s.c("vcw"))]
            if dc == 0:
                wt += vfirst + [(ln["apd"], ln["apd_full"])]
            s.add("tensor", lambda e, dc=dc, tc=tc: e.matmul(
                pbanks[tc][:], ACT[:, TLOC * dc + 128 * tc: TLOC * dc + 128 * (tc + 1)],
                wvbf[dc % 2][:], start=(dc == 0), stop=(dc == NCH - 1)),
                waits=wt, inc=("vpm", 1))
    for tc in range(8):
        w = [("vpm", s.c("vpm"))]
        if tc >= 2:
            w.append(("vso", 16 * (tc - 1)))
        s.add("scalar", lambda e, tc=tc: e.activation(vst[tc % 2][:], pbanks[tc][:], AF.Copy),
              waits=w, inc=("vpe", 1))
        s.add("gpsimd", lambda e, tc=tc: e.dma_start(out=VL[128 * tc:128 * (tc + 1), :], in_=vst[tc % 2][:]),
              waits=[("vpe", s.c("vpe"))], inc=("vso", 16))
    s.add("gpsimd", lambda e: e.nop(),
          waits=[(pj["so"], pj["so_full"]), ("vso", s.c("vso"))])

    s.realize(nc, es, block)
    es.close()
    lower_extended_insts(nc)
    return nc



def core_token_slice(core):
    r = core % 4
    return [r + 4 * i for i in range(8)]


def shard_xT(x):
    out = []
    for core in range(NCORE):
        b = core // 4
        chunks = core_token_slice(core)
        xs = np.concatenate([x[b, 128 * g:128 * (g + 1), :] for g in chunks], axis=0)
        out.append(np.ascontiguousarray(xs.T))
    return out


def unshard_T(parts, dtype=np.float32):
    full = np.empty((B, S, D), dtype=dtype)
    for core in range(NCORE):
        b = core // 4
        for li, g in enumerate(core_token_slice(core)):
            full[b, 128 * g:128 * (g + 1), :] = np.asarray(parts[core][:, 128 * li:128 * (li + 1)]).T
    return full


def _assemble_T(parts, rows, dtype):
    full = [np.empty((rows, S), dtype=dtype) for _ in range(B)]
    for core in range(NCORE):
        b = core // 4
        for li, g in enumerate(core_token_slice(core)):
            full[b][:, 128 * g:128 * (g + 1)] = parts[core][:, 128 * li:128 * (li + 1)]
    return full


def _reshard_T(full, dtype=None):
    out = []
    for core in range(NCORE):
        b = core // 4
        cols = np.concatenate([np.arange(128 * g, 128 * (g + 1)) for g in core_token_slice(core)])
        a = full[b][:, cols]
        out.append(np.ascontiguousarray(a if dtype is None else a.astype(dtype)))
    return out


def kernel_multi(L0_attn_x, L0_attn_Wq, L0_attn_Wk, L0_attn_Wv, L0_attn_Wo,
           L0_ffn_W, L0_ffn_b, L0_ffn_A, L0_ffn_B, L0_ffn_p_scores,
           L1_attn_w_qkv, L1_attn_w_c,
           L1_ffn_W_base, L1_ffn_b_base, L1_ffn_W_router, L1_ffn_b_router,
           L1_ffn_lora_A, L1_ffn_lora_B):
    f32 = np.float32
    asf = lambda a: np.ascontiguousarray(np.asarray(a, dtype=f32))
    x0 = asf(L0_attn_x)
    xT_shards = K_shard = shard_xT(x0)

    ncA = build_launchA()
    imA = [{"xT": xT_shards[c], "Wq": asf(L0_attn_Wq), "Wk": asf(L0_attn_Wk),
            "Wv": asf(L0_attn_Wv)} for c in range(NCORE)]
    rA = run_bass_kernel_spmd(ncA, imA, core_ids=list(range(NCORE)))
    QTf = _assemble_T([rA.results[c]["QT"] for c in range(NCORE)], 4096, BF)
    KTf = _assemble_T([rA.results[c]["KT"] for c in range(NCORE)], 512, BF)
    VLf = [np.ascontiguousarray(v.T) for v in _assemble_T(
        [np.ascontiguousarray(rA.results[c]["VL"].T) for c in range(NCORE)], 512, BF)]

    ncB = build_launchB()
    imB = []
    for core in range(NCORE):
        b, g = core // 4, core % 4
        imB.append({"QTg": np.ascontiguousarray(QTf[b][1024 * g:1024 * (g + 1), :]),
                    "KTg": np.ascontiguousarray(KTf[b][128 * g:128 * (g + 1), :]),
                    "Vg": np.ascontiguousarray(VLf[b][:, 128 * g:128 * (g + 1)])})
    rB = run_bass_kernel_spmd(ncB, imB, core_ids=list(range(NCORE)))
    OTf = [np.concatenate([rB.results[4 * b + g]["OTg"] for g in range(4)], axis=0)
           for b in range(B)]
    OT_shards = _reshard_T(OTf)

    ncC = build_launchC()
    WT0 = asf(np.asarray(L0_ffn_W).T)
    bb0 = asf(np.asarray(L0_ffn_b).reshape(32, 128).T)
    Aall0 = asf(np.asarray(L0_ffn_A).transpose(1, 0, 2).reshape(4096, 128))
    Bflat0 = asf(np.asarray(L0_ffn_B).reshape(128, 4096))
    prep0 = asf(np.repeat(np.asarray(L0_ffn_p_scores), 16).reshape(128, 1))
    imC = [{"x0T": xT_shards[c], "OT": OT_shards[c], "Wo": asf(L0_attn_Wo),
            "WT0": WT0, "bb0": bb0, "Aall0": Aall0, "Bflat0": Bflat0,
            "prep0": prep0, "Wqkv": asf(L1_attn_w_qkv)} for c in range(NCORE)]
    rC = run_bass_kernel_spmd(ncC, imC, core_ids=list(range(NCORE)))
    x2_shards = [rC.results[c]["x2T"] for c in range(NCORE)]
    QT1f = _assemble_T([rC.results[c]["QT1"] for c in range(NCORE)], 4096, BF)
    kTf = _assemble_T([rC.results[c]["kT1"] for c in range(NCORE)], 128, BF)
    vTf = _assemble_T([rC.results[c]["vT1"] for c in range(NCORE)], 128, BF)
    vLf = [np.ascontiguousarray(v.T) for v in vTf]

    ncD = build_launchD()
    slopes = np.array(ALIBI_SLOPES, dtype=f32)
    imD = []
    for core in range(NCORE):
        b, g = core // 4, core % 4
        imD.append({"QTg": np.ascontiguousarray(QT1f[b][1024 * g:1024 * (g + 1), :]),
                    "kT": kTf[b], "vL": vLf[b],
                    "slop": np.ascontiguousarray(np.tile(slopes[8 * g:8 * (g + 1)], (128, 1)))})
    rD = run_bass_kernel_spmd(ncD, imD, core_ids=list(range(NCORE)))
    OT1f = [np.concatenate([rD.results[4 * b + g]["OTg"] for g in range(4)], axis=0)
            for b in range(B)]
    OT1_shards = _reshard_T(OT1f)

    ncE = build_launchE()
    WTb = asf(np.asarray(L1_ffn_W_base).T)
    bbb = asf(np.asarray(L1_ffn_b_base).reshape(32, 128).T)
    WrT = asf(np.asarray(L1_ffn_W_router).T)
    brt = asf(np.asarray(L1_ffn_b_router).reshape(8, 1))
    AallL = asf(np.asarray(L1_ffn_lora_A).transpose(1, 0, 2).reshape(4096, 128))
    BflatL = asf(np.asarray(L1_ffn_lora_B).reshape(128, 4096))
    imE = [{"x2T": x2_shards[c], "OT1": OT1_shards[c], "Wc": asf(L1_attn_w_c),
            "WTb": WTb, "bbb": bbb, "WrT": WrT, "brt": brt, "AallL": AallL,
            "BflatL": BflatL, "id8": np.eye(8, dtype=f32),
            "id128": np.eye(128, dtype=f32)} for c in range(NCORE)]
    rE = run_bass_kernel_spmd(ncE, imE, core_ids=list(range(NCORE)))
    return unshard_T([rE.results[c]["xoutT"] for c in range(NCORE)], dtype=f32)



def build_launchB():
    nc = bass.Bass("TRN2", target_bir_lowering=False)
    QTg = nc.dram_tensor("QTg", [1024, S], BF16, kind="ExternalInput").ap()
    KTg = nc.dram_tensor("KTg", [128, S], BF16, kind="ExternalInput").ap()
    Vg = nc.dram_tensor("Vg", [S, 128], BF16, kind="ExternalInput").ap()
    OTg = nc.dram_tensor("OTg", [1024, S], BF16, kind="ExternalOutput").ap()

    es = ExitStack()
    sb, ps = _sb(nc, es), _ps(nc, es)
    KTs = sb("KTs", [128, S], BF16)
    Vs = sb("Vs", [128, S], BF16)
    ones128 = sb("ones128b", [128, 128], BF16)
    qt = [sb(f"qt{i}", [128, 512], BF16) for i in range(2)]
    Eb = [sb(f"Eb{i}", [128, 512], BF16) for i in range(2)]
    vsum = sb("vsum", [1, 128], BF16)
    rs = sb("rs", [128, 512])
    ost = [sb(f"ostb{i}", [128, 512], BF16) for i in range(2)]
    pS = [ps(f"pS{i}", [128, 512]) for i in range(2)]
    pA = ps("pA", [128, 512])
    pR = ps("pR", [128, 512])
    pVs = ps("pVs", [1, 128])

    s = Sched()
    block = es.enter_context(nc.Block())

    s.add("gpsimd", lambda e: e.memset(ones128[:], 1.0), inc=("init", 1))
    s.add("sync", lambda e: e.dma_start(out=KTs[:], in_=KTg), inc=("ld", 16))
    s.add("sync", lambda e: e.dma_start(
        out=Vs[:].rearrange("p (kc d) -> p kc d", kc=32),
        in_=Vg.rearrange("(kc p) d -> p kc d", kc=32)), inc=("ld", 16))
    for kc in range(32):
        s.add("tensor", lambda e, kc=kc: e.matmul(
            pVs[:], ones128[:, 0:1], Vs[:, 128 * kc:128 * (kc + 1)],
            start=(kc == 0), stop=(kc == 31)),
            waits=[("init", 1), ("ld", 32)], inc=(("vsm", 1) if kc == 31 else None))
    s.add("scalar", lambda e: e.activation(vsum[:], pVs[:], AF.Copy),
          waits=[("vsm", 1)], inc=("vse", 1))

    for h in range(8):
        for qb in range(8):
            kcs = list(range(4 * qb, 32))
            nkc = len(kcs)
            wq = [("ld", 32)]
            if s.c("qld") >= 32:
                wq.append(("smm", s.c("smm")))
            s.add("sync", lambda e, h=h, qb=qb: e.dma_start(
                out=qt[qb % 2][:], in_=QTg[128 * h:128 * (h + 1), 512 * qb:512 * (qb + 1)]),
                waits=wq, inc=("qld", 16))
            qld_now = s.c("qld")

            def emit_S(idx, kc, h=h, qb=qb, qld_now=qld_now):
                sbk = idx % 2
                w = [("qld", qld_now), ("vse", 1)]
                if s.c("smm") >= 2:
                    w.append(("avm", s.c("avm") - 2))
                    w.append(("expd", s.c("expd") - 1))
                s.add("tensor", lambda e, kc=kc, sbk=sbk, qb=qb, h=h: e.matmul(
                    pS[sbk][:], KTs[:, 128 * kc:128 * (kc + 1)], qt[qb % 2][:],
                    start=True, stop=True), waits=w, inc=("smm", 1))
                we = [("smm", s.c("smm"))]
                if s.c("avm") > 0:
                    we.append(("avm", s.c("avm")))
                s.add("scalar", lambda e, sbk=sbk: e.activation(
                    Eb[sbk][:], pS[sbk][:], AF.Exp), waits=we, inc=("expd", 1))
                if kc <= 4 * qb + 3:
                    s.add("gpsimd", lambda e, sbk=sbk, kc=kc, qb=qb: e.affine_select(
                        out=Eb[sbk][:], in_=Eb[sbk][:], compare_op=ALU.is_ge, fill=0.0,
                        base=128 * kc - 512 * qb - 1, channel_multiplier=1,
                        pattern=[[-1, 512]]), waits=[("expd", s.c("expd"))], inc=("affd", 1))
                    return ("affd", s.c("affd"))
                return ("expd", s.c("expd"))

            def emit_AV(idx, kc, ready, first, last, h=h, qb=qb):
                sbk = idx % 2
                wav = [ready]
                if first and s.c("nrm") > 0:
                    wav.append(("nrm", s.c("nrm")))
                s.add("tensor", lambda e, kc=kc, sbk=sbk: e.matmul(
                    pA[:], Vs[:, 128 * kc:128 * (kc + 1)], Eb[sbk][:],
                    start=first, stop=False), waits=wav, inc=("avm", 1))
                s.add("tensor", lambda e, sbk=sbk: e.matmul(
                    pR[:], ones128[:], Eb[sbk][:],
                    start=first, stop=last), inc=("avm", 1))

            prev = None
            for idx, kc in enumerate(kcs):
                ready = emit_S(idx, kc)
                if prev is not None:
                    emit_AV(prev[0], prev[1], prev[2], prev[0] == 0, False)
                prev = (idx, kc, ready)
            emit_AV(prev[0], prev[1], prev[2], prev[0] == 0, qb != 7)
            if qb == 7:
                s.add("tensor", lambda e: e.matmul(
                    pA[0:128, 511:512], vsum[:], ones128[0:1, 0:1],
                    start=False, stop=True), waits=[("vse", 1)], inc=("avm", 1))
            avm_done = s.c("avm")
            if qb == 7:
                s.add("vector", lambda e: e.memset(pR[:, 511:512], 4096.0),
                      waits=[("avm", avm_done)])
            wn = [("avm", avm_done)]
            if s.c("oso") >= 32:
                wn.append(("oso", s.c("oso") - 16))
            s.add("vector", lambda e: e.reciprocal(rs[:], pR[:]), waits=wn)
            s.add("vector", lambda e, h=h, qb=qb: e.tensor_mul(
                ost[(8 * h + qb) % 2][:], pA[:], rs[:]), inc=("nrm", 1))
            s.add("gpsimd", lambda e, h=h, qb=qb: e.dma_start(
                out=OTg[128 * h:128 * (h + 1), 512 * qb:512 * (qb + 1)],
                in_=ost[(8 * h + qb) % 2][:]), waits=[("nrm", s.c("nrm"))], inc=("oso", 16))

    s.add("gpsimd", lambda e: e.nop(), waits=[("oso", s.c("oso"))])
    s.realize(nc, es, block)
    es.close()
    lower_extended_insts(nc)
    return nc



def emit_load_act(s, act_ap, src_ap, tag, ntok=TLOC):
    for c in range(NCH):
        s.add("sync", lambda e, c=c: e.dma_start(
            out=act_ap[:, ntok * c:ntok * (c + 1)], in_=src_ap[128 * c:128 * (c + 1), :]),
            inc=(tag + "ld", 16))
    return {"ld": tag + "ld", "full": s.c(tag + "ld")}


def emit_stats_finalize(s, bufs, tag, wait=None):
    rt, mrt, scr, epsc = bufs["rt"], bufs["mrt"], bufs["scr"], bufs["epsc"]
    pSx, pSxx = bufs["pSx"], bufs["pSxx"]
    sfin = tag + "fin"
    for t in range(2):
        s.add("vector", lambda e, t=t: e.tensor_scalar_mul(scr[t][:], pSx[t][:], 1.0 / D),
              waits=[wait] if (t == 0 and wait is not None) else [])
        s.add("vector", lambda e, t=t: e.tensor_scalar_mul(scr[2 + t][:], pSxx[t][:], 1.0 / D))
        s.add("vector", lambda e, t=t: e.tensor_mul(rt[t][:], scr[t][:], scr[t][:]))
        s.add("vector", lambda e, t=t: e.tensor_sub(scr[2 + t][:], scr[2 + t][:], rt[t][:]),
              inc=(sfin, 1))
        s.add("scalar", lambda e, t=t: e.activation(scr[2 + t][:], scr[2 + t][:], AF.Sqrt, bias=bufs["epsc"][:]),
              waits=[(sfin, s.c(sfin))], inc=(sfin, 1))
        s.add("vector", lambda e, t=t: e.reciprocal(rt[t][:], scr[2 + t][:]),
              waits=[(sfin, s.c(sfin))])
        s.add("vector", lambda e, t=t: e.tensor_mul(mrt[t][:], scr[t][:], rt[t][:]),
              inc=(sfin, 1))
    return {"fin": sfin, "full": s.c(sfin)}


def emit_apply(s, bufs, src_ap, act_ap, tag, ntok=TLOC, gate=()):
    xs, rt, mrt, aps = bufs["xs"], bufs["rt"], bufs["mrt"], bufs["aps"]
    ld, apd = tag + "ld", tag + "ap"
    ntb = ntok // 512
    b_apd = s.c(apd)
    for c in range(NCH):
        w = list(gate) if c < 2 else []
        if c >= 2:
            w.append((apd, b_apd + ntb * (c - 1)))
        s.add("sync", lambda e, c=c: e.dma_start(out=xs[c % 2][:], in_=src_ap[128 * c:128 * (c + 1), :]),
              waits=w, inc=(ld, 16))
        for t in range(ntb):
            colp = slice(512 * t, 512 * (t + 1))
            cola = slice(ntok * c + 512 * t, ntok * c + 512 * (t + 1))
            s.add("vector", lambda e, c=c, t=t, colp=colp: e.tensor_mul(
                aps[t][:], xs[c % 2][:, colp], rt[t][:]), waits=[(ld, s.c(ld))])
            s.add("vector", lambda e, c=c, t=t, cola=cola: e.tensor_tensor(
                act_ap[:, cola], aps[t][:], mrt[t][:], op=ALU.subtract), inc=(apd, 1))
    return {"apd": apd, "full": s.c(apd)}


def emit_combine(s, bufs, tag, loads, mode, out_ap, stats="meanvar", ntok=TLOC, gate=(),
                 gam=None, extra_vec_wait=None, stats_gate=None):
    res, resb, sqs = bufs["res"], bufs["resb"], bufs["sq"]
    ones128 = bufs["ones128"]
    pSx, pSxx = bufs["pSx"], bufs["pSxx"]
    ld, vcd, cst, sqd, smm, sod = (tag + n for n in ("ld", "vc", "ca", "sq", "smm", "so"))
    ntb = ntok // 512
    b_sod = s.c(sod)
    for c in range(NCH):
        w = list(gate) if c < 2 else []
        if c >= 2:
            w.append((sod, b_sod + 16 * (c - 1)))
            if stats and stats != "none":
                w.append((cst, s.c(cst) - 1))
                w.append((sqd, s.c(sqd) - 1))
        for (ap_, bp) in loads:
            s.add("sync", lambda e, c=c, ap_=ap_, bp=bp: e.dma_start(
                out=bp[c % 2][:], in_=ap_[128 * c:128 * (c + 1), :]),
                waits=w, inc=(ld, 16))
            w = []
        tiles = [bp for (_, bp) in loads]
        aps, rt, mrt = bufs["aps"], bufs["rt"], bufs["mrt"]
        for t in range(ntb):
            colp = slice(512 * t, 512 * (t + 1))
            wv = [(ld, s.c(ld))]
            if extra_vec_wait is not None:
                wv.append(extra_vec_wait)
            if mode == "postln":
                zt, xt = tiles[0], tiles[1]
                s.add("vector", lambda e, c=c, t=t, colp=colp, zt=zt: e.tensor_mul(
                    aps[t][:], zt[c % 2][:, colp], rt[t][:]), waits=wv)
                s.add("vector", lambda e, t=t: e.tensor_sub(aps[t][:], aps[t][:], mrt[t][:]))
                s.add("vector", lambda e, c=c, t=t, colp=colp, xt=xt: e.tensor_add(
                    res[c % 2][:, colp], xt[c % 2][:, colp], aps[t][:]), inc=(vcd, 1))
            elif mode == "add":
                at, bt = tiles[0], tiles[1]
                s.add("vector", lambda e, c=c, t=t, colp=colp, at=at, bt=bt: e.tensor_add(
                    res[c % 2][:, colp], at[c % 2][:, colp], bt[c % 2][:, colp]),
                    waits=wv, inc=(vcd, 1))
            elif mode == "gate":
                zt, mt = tiles[0], tiles[1]
                s.add("vector", lambda e, c=c, t=t, colp=colp, mt=mt: e.tensor_mul(
                    aps[t][:], mt[c % 2][:, colp], gam[t][:]), waits=wv)
                s.add("vector", lambda e, c=c, t=t, colp=colp, zt=zt: e.tensor_add(
                    res[c % 2][:, colp], zt[c % 2][:, colp], aps[t][:]), inc=(vcd, 1))
            else:
                raise ValueError(mode)
        s.add("gpsimd", lambda e, c=c: e.dma_start(
            out=out_ap[128 * c:128 * (c + 1), :], in_=res[c % 2][:]),
            waits=[(vcd, s.c(vcd))], inc=(sod, 16))
        if stats and stats != "none":
            wst_ = [(vcd, s.c(vcd))]
            if c >= 2:
                wst_.append((smm, s.c(smm) - 2 * ntb))
            s.add("gpsimd", lambda e, c=c: e.tensor_copy(resb[c % 2][:], res[c % 2][:]),
                  waits=wst_, inc=(cst, 1))
            s.add("scalar", lambda e, c=c: e.activation(sqs[c % 2][:], res[c % 2][:], AF.Square),
                  waits=wst_, inc=(sqd, 1))
            for t in range(ntb):
                cs = slice(512 * t, 512 * (t + 1))
                wm1 = [(cst, s.c(cst))]
                wm2 = [(sqd, s.c(sqd))]
                if c == 0 and stats_gate is not None:
                    wm1.append(stats_gate)
                    wm2.append(stats_gate)
                s.add("tensor", lambda e, c=c, t=t, cs=cs: e.matmul(
                    pSx[t][:], ones128[:], resb[c % 2][:, cs], start=(c == 0), stop=(c == NCH - 1)),
                    waits=wm1, inc=(smm, 1))
                s.add("tensor", lambda e, c=c, t=t, cs=cs: e.matmul(
                    pSxx[t][:], ones128[:], sqs[c % 2][:, cs], start=(c == 0), stop=(c == NCH - 1)),
                    waits=wm2, inc=(smm, 1))
    return {"so": sod, "so_full": s.c(sod), "smm": smm, "smm_full": s.c(smm)}


def emit_proj_spill(s, bufs, specs, act_ap, tag, spill_ap, stats="meanvar", ntok=TLOC, gate=(), stats_gate=None):
    wst, wbf = bufs["wst"], bufs["wbf"]
    zf, sqs, zb = bufs["zf"], bufs["sq"], bufs["resb"]
    ones128 = bufs["ones128"]
    pP = bufs["pP"]
    pSx, pSxx = bufs["pSx"], bufs["pSxx"]
    lw, cw, pm, pe, cst, sqd, smm, sod = (tag + n for n in
                                          ("lw", "cw", "pm", "pe", "ca", "sq", "smm", "so"))
    ntb = ntok // 512
    b_pm, b_pe, b_sod = s.c(pm), s.c(pe), s.c(sod)
    single_wst = wst[0] is wst[1]
    flat = [(sp, j) for sp in specs for j in range(sp["oc"])]
    nw = len(flat)
    for ow, (sp, j) in enumerate(flat):
        if ow < 2:
            w = list(gate) if ow == 0 else ([(cw, s.c(cw))] if single_wst else [])
        else:
            w = [(cw, s.c(cw) if single_wst else s.c(cw) - 1)]
        s.add("sync", lambda e, ow=ow, sp=sp, j=j: e.dma_start(
            out=wst[ow % 2][:].rearrange("p (c o) -> p c o", c=NCH),
            in_=sp["w"][:, 128 * j:128 * (j + 1)].rearrange("(c p) o -> p c o", c=NCH)),
            waits=w, inc=(lw, 16))
        wc = [(lw, s.c(lw))]
        if ow >= 2:
            wc.append((pm, b_pm + ntb * (ow - 1)))
        s.add("gpsimd", lambda e, ow=ow: e.tensor_copy(wbf[ow % 2][:], wst[ow % 2][:]),
              waits=wc, inc=(cw, 1))
        for tb in range(ntb):
            bank = (ow % 2) * ntb + tb
            for dc in range(NCH):
                wt = []
                if dc == 0:
                    wt = [(cw, s.c(cw))]
                    if ow < 2:
                        wt += list(gate)
                    if ow >= 2:
                        wt.append((pe, b_pe + ntb * (ow - 2) + tb + 1))
                last = (dc == NCH - 1)
                s.add("tensor", lambda e, ow=ow, tb=tb, dc=dc, last=last, bank=bank: e.matmul(
                    pP[bank][:], wbf[ow % 2][:, 128 * dc:128 * (dc + 1)],
                    act_ap[:, ntok * dc + 512 * tb: ntok * dc + 512 * (tb + 1)],
                    start=(dc == 0), stop=last),
                    waits=wt, inc=((pm, 1) if last else None))
            ev_w = [(pm, s.c(pm))]
            if ow >= 2 and tb == 0:
                ev_w.append((sod, b_sod + 16 * (ow - 1)))
                if stats != "none":
                    ev_w.append((cst, s.c(cst) - 1))
            if sp.get("bias") is not None:
                bias_ap = sp["bias"](j)
                s.add("vector", lambda e, ow=ow, tb=tb, bank=bank, bias_ap=bias_ap: e.tensor_scalar_add(
                    zf[ow % 2][:, 512 * tb:512 * (tb + 1)], pP[bank][:], bias_ap),
                    waits=ev_w, inc=(pe, 1))
            else:
                s.add("scalar", lambda e, ow=ow, tb=tb, bank=bank, sp=sp: e.activation(
                    zf[ow % 2][:, 512 * tb:512 * (tb + 1)], pP[bank][:], AF.Copy,
                    scale=sp.get("scale", 1.0)), waits=ev_w, inc=(pe, 1))
            if stats != "none":
                smm_per_ow = (2 if stats == "meanvar" else 1) * ntb
                wsq = [(pe, s.c(pe))]
                if ow >= 2:
                    wsq.append((smm, s.c(smm) - smm_per_ow))
                s.add("scalar", lambda e, ow=ow, tb=tb: e.activation(
                    sqs[ow % 2][:, 512 * tb:512 * (tb + 1)], zf[ow % 2][:, 512 * tb:512 * (tb + 1)],
                    AF.Square), waits=wsq, inc=(sqd, 1))
        if stats != "none":
            smm_per_ow = (2 if stats == "meanvar" else 1) * ntb
            wzb = [(pe, s.c(pe))]
            if ow >= 2:
                wzb.append((smm, s.c(smm) - smm_per_ow))
            s.add("gpsimd", lambda e, ow=ow: e.tensor_copy(zb[ow % 2][:], zf[ow % 2][:]),
                  waits=wzb, inc=(cst, 1))
        wso = [(pe, s.c(pe))]
        s.add("gpsimd", lambda e, ow=ow: e.dma_start(
            out=spill_ap[128 * ow:128 * (ow + 1), :], in_=zf[ow % 2][:]),
            waits=wso, inc=(sod, 16))
        for t in range(ntb if stats != "none" else 0):
            cs = slice(512 * t, 512 * (t + 1))
            wm1 = [(cst, s.c(cst))]
            wm2 = [(cst, s.c(cst)), (sqd, s.c(sqd))]
            if ow == 0 and stats_gate is not None:
                wm1.append(stats_gate)
                wm2.append(stats_gate)
            if stats == "meanvar":
                s.add("tensor", lambda e, ow=ow, t=t, cs=cs: e.matmul(
                    pSx[t][:], ones128[:], zb[ow % 2][:, cs], start=(ow == 0), stop=(ow == nw - 1)),
                    waits=wm1, inc=(smm, 1))
            s.add("tensor", lambda e, ow=ow, t=t, cs=cs: e.matmul(
                pSxx[t][:], ones128[:], sqs[ow % 2][:, cs], start=(ow == 0), stop=(ow == nw - 1)),
                waits=wm2, inc=(smm, 1))
    return {"pm": pm, "pm_full": s.c(pm), "smm": smm, "smm_full": s.c(smm),
            "so": sod, "so_full": s.c(sod)}



def _mk_bufsC(nc, es):
    sb, ps = _sb(nc, es), _ps(nc, es)
    bufs = {
        "xs": [sb(f"xs{i}", [128, TLOC]) for i in range(2)],
        "xb": [sb(f"xb{i}", [128, TLOC], BF16) for i in range(2)],
        "sq": [sb(f"sq{i}", [128, TLOC], BF16) for i in range(2)],
        "ones128": sb("ones128", [128, 128], BF16),
        "epsc": sb("epsc", [128, 1]),
        "rt": [sb(f"rt{t}", [128, 512]) for t in range(2)],
        "mrt": [sb(f"mrt{t}", [128, 512]) for t in range(2)],
        "scr": [sb(f"scr{t}", [128, 512]) for t in range(4)],
        "aps": [sb(f"aps{t}", [128, 512]) for t in range(2)],
        "wst": [sb(f"wst{i}", [128, 4096]) for i in range(2)],
        "wbf": [sb(f"wbf{i}", [128, 4096], BF16) for i in range(2)],
        "ost": [sb(f"ost{i}", [128, TLOC], BF16) for i in range(2)],
        "zf": [sb(f"zf{i}", [128, TLOC]) for i in range(2)],
        "res": [sb(f"res{i}", [128, TLOC]) for i in range(2)],
        "resb": [sb(f"resb{i}", [128, TLOC], BF16) for i in range(2)],
    }
    pbanks = [ps(f"pb{i}", [128, 512]) for i in range(8)]
    bufs["pP"] = pbanks[0:4]
    bufs["pSx"] = pbanks[4:6]
    bufs["pSxx"] = pbanks[6:8]
    bufs["pbanks"] = pbanks
    return bufs


def emit_m_path(s, bufs, Aall, Bflat, prep_sb, act_ap, mspill, tag, gate=()):
    wst, wbf, zf, sqs, zb = bufs["wst"], bufs["wbf"], bufs["zf"], bufs["sq"], bufs["resb"]
    ones128, ub = bufs["ones128"], bufs["ub"]
    pP, pSxx = bufs["pP"], bufs["pSxx"]
    lw, cw, pm, pe, cst, sqd, smm, sod, ue = (tag + n for n in
        ("lw", "cw", "pm", "pe", "ca", "sq", "smm", "so", "ue"))
    b_pe, b_sod, b_smm, b_ue = s.c(pe), s.c(sod), s.c(smm), s.c(ue)
    b_lw, b_cw = s.c(lw), s.c(cw)
    s.add("sync", lambda e: e.dma_start(
        out=wst[0][:].rearrange("p (c o) -> p c o", c=NCH),
        in_=Aall.rearrange("(c p) o -> p c o", c=NCH)), waits=list(gate), inc=(lw, 16))
    s.add("gpsimd", lambda e: e.tensor_copy(wbf[0][:], wst[0][:]),
          waits=[(lw, b_lw + 16)], inc=(cw, 1))
    s.add("sync", lambda e: e.dma_start(out=wst[1][:], in_=Bflat),
          waits=[(cw, b_cw + 1)] if wst[0] is wst[1] else [], inc=(lw, 16))
    s.add("gpsimd", lambda e: e.tensor_copy(wbf[1][:], wst[1][:]),
          waits=[(lw, b_lw + 32)], inc=(cw, 1))
    for tb in range(2):
        for dc in range(NCH):
            wt = [(cw, b_cw + 1)] + (list(gate) if dc == 0 else [])
            s.add("tensor", lambda e, tb=tb, dc=dc: e.matmul(
                pP[tb][:], wbf[0][:, 128 * dc:128 * (dc + 1)],
                act_ap[:, TLOC * dc + 512 * tb: TLOC * dc + 512 * (tb + 1)],
                start=(dc == 0), stop=(dc == NCH - 1)),
                waits=wt, inc=((pm, 1) if dc == NCH - 1 else None))
        s.add("scalar", lambda e, tb=tb: e.activation(
            ub[:, 512 * tb:512 * (tb + 1)], pP[tb][:], AF.Copy, scale=prep_sb[:, 0:1]),
            waits=[(pm, s.c(pm))], inc=(ue, 1))
    for ow in range(NCH):
        for tb in range(2):
            bank = (ow % 2) * 2 + tb
            wt = [(cw, b_cw + 2), (ue, b_ue + 2)]
            if ow >= 2:
                wt.append((pe, b_pe + 2 * (ow - 2) + tb + 1))
            s.add("tensor", lambda e, ow=ow, tb=tb, bank=bank: e.matmul(
                pP[bank][:], wbf[1][:, 128 * ow:128 * (ow + 1)],
                ub[:, 512 * tb:512 * (tb + 1)], start=True, stop=True),
                waits=wt, inc=(pm, 1))
            ev_w = [(pm, s.c(pm))]
            if ow >= 2 and tb == 0:
                ev_w += [(sod, b_sod + 16 * (ow - 1)), (cst, s.c(cst) - 1)]
            s.add("scalar", lambda e, ow=ow, tb=tb, bank=bank: e.activation(
                zf[ow % 2][:, 512 * tb:512 * (tb + 1)], pP[bank][:], AF.Copy), waits=ev_w, inc=(pe, 1))
            wsq = [(smm, s.c(smm) - 2)] if ow >= 2 else []
            wsq.append((pe, s.c(pe)))
            s.add("scalar", lambda e, ow=ow, tb=tb: e.activation(
                sqs[ow % 2][:, 512 * tb:512 * (tb + 1)], zf[ow % 2][:, 512 * tb:512 * (tb + 1)],
                AF.Square), waits=wsq, inc=(sqd, 1))
        s.add("gpsimd", lambda e, ow=ow: e.dma_start(
            out=mspill[128 * ow:128 * (ow + 1), :], in_=zf[ow % 2][:]),
            waits=[(pe, s.c(pe))], inc=(sod, 16))
        for t in range(2):
            cs = slice(512 * t, 512 * (t + 1))
            s.add("tensor", lambda e, ow=ow, t=t, cs=cs: e.matmul(
                bufs["pSxx"][t][:], ones128[:], sqs[ow % 2][:, cs],
                start=(ow == 0), stop=(ow == NCH - 1)),
                waits=[(sqd, s.c(sqd))], inc=(smm, 1))
    return {"so": sod, "so_full": s.c(sod), "smm": smm, "smm_full": s.c(smm)}


def emit_gate(s, bufs, nm2, gam, tag, parts=6, wait=None):
    scr, aps = bufs["scr"], bufs["aps"]
    g = tag + "g"
    for t in range(2):
        if parts >= 1:
            s.add("vector", lambda e, t=t: e.tensor_copy(scr[t][:], bufs["pSxx"][t][:]),
                  waits=[wait] if (t == 0 and wait is not None) else [], inc=(g, 1))
        else:
            s.add("vector", lambda e, t=t: e.tensor_copy(scr[t][:], nm2[t][:]),
                  inc=(g, 1))
        if parts >= 2:
            s.add("scalar", lambda e, t=t: e.activation(aps[t][:], scr[t][:], AF.Sqrt),
                  waits=[(g, s.c(g))], inc=(g, 1))
        else:
            s.add("vector", lambda e, t=t: e.tensor_copy(aps[t][:], scr[t][:]),
                  waits=[(g, s.c(g))], inc=(g, 1))
        if parts >= 3:
            s.add("scalar", lambda e, t=t: e.activation(scr[2 + t][:], nm2[t][:], AF.Sqrt),
                  inc=(g, 1))
        else:
            s.add("vector", lambda e, t=t: e.tensor_copy(scr[2 + t][:], nm2[t][:]),
                  inc=(g, 1))
        if parts >= 4:
            s.add("vector", lambda e, t=t: e.tensor_scalar_add(scr[2 + t][:], scr[2 + t][:], 1e-6),
                  waits=[(g, s.c(g))])
            s.add("vector", lambda e, t=t: e.reciprocal(scr[2 + t][:], scr[2 + t][:]))
        if parts >= 5:
            s.add("vector", lambda e, t=t: e.tensor_mul(gam[t][:], aps[t][:], scr[2 + t][:]),
                  waits=[(g, s.c(g))])
            s.add("vector", lambda e, t=t: e.tensor_scalar(gam[t][:], gam[t][:], 0.5, 1.0,
                                                           op0=ALU.mult, op1=ALU.min),
                  inc=(g, 1))
        else:
            s.add("vector", lambda e, t=t: e.tensor_copy(gam[t][:], scr[2 + t][:]),
                  waits=[(g, s.c(g))], inc=(g, 1))
    return {"g": g, "full": s.c(g)}


def build_launchC(upto=10000, gate_parts=6):
    nc = bass.Bass("TRN2", target_bir_lowering=False)
    x0T = nc.dram_tensor("x0T", [D, TLOC], F32, kind="ExternalInput").ap()
    OT = nc.dram_tensor("OT", [D, TLOC], BF16, kind="ExternalInput").ap()
    Wo = nc.dram_tensor("Wo", [D, D], F32, kind="ExternalInput").ap()
    WT0 = nc.dram_tensor("WT0", [D, D], F32, kind="ExternalInput").ap()
    bb0 = nc.dram_tensor("bb0", [128, NCH], F32, kind="ExternalInput").ap()
    Aall0 = nc.dram_tensor("Aall0", [D, 128], F32, kind="ExternalInput").ap()
    Bflat0 = nc.dram_tensor("Bflat0", [128, D], F32, kind="ExternalInput").ap()
    prep0 = nc.dram_tensor("prep0", [128, 1], F32, kind="ExternalInput").ap()
    Wqkv = nc.dram_tensor("Wqkv", [D, 4352], F32, kind="ExternalInput").ap()
    x2T = nc.dram_tensor("x2T", [D, TLOC], F32, kind="ExternalOutput").ap()
    QT1 = nc.dram_tensor("QT1", [4096, TLOC], BF16, kind="ExternalOutput").ap()
    kT1 = nc.dram_tensor("kT1", [128, TLOC], BF16, kind="ExternalOutput").ap()
    vT1 = nc.dram_tensor("vT1", [128, TLOC], BF16, kind="ExternalOutput").ap()
    aspill = nc.dram_tensor("aspill", [D, TLOC], F32, kind="Internal").ap()
    x1T = nc.dram_tensor("x1T", [D, TLOC], F32, kind="Internal").ap()
    mspill = nc.dram_tensor("mspill", [D, TLOC], F32, kind="Internal").ap()
    zspill = nc.dram_tensor("zspill", [D, TLOC], F32, kind="Internal").ap()
    fspill = nc.dram_tensor("fspill", [D, TLOC], F32, kind="Internal").ap()

    es = ExitStack()
    bufs = _mk_bufsC(nc, es)
    sb = _sb(nc, es)
    ACT = sb("ACT", [128, NCH * TLOC], BF16)
    bufs["ub"] = pv(49152, 2048, BF16, 1024)
    bbs = sb("bbs", [128, NCH])
    preps = sb("preps", [128, 1])
    nm2 = [sb(f"nm2{t}", [128, 512]) for t in range(2)]
    gam = [sb(f"gam{t}", [128, 512]) for t in range(2)]

    s = Sched()
    block = es.enter_context(nc.Block())
    s.add("gpsimd", lambda e: e.memset(bufs["epsc"][:], LN_EPS), inc=("init", 1))
    s.add("gpsimd", lambda e: e.memset(bufs["ones128"][:], 1.0), inc=("init", 1))
    s.add("sync", lambda e: e.dma_start(out=bbs[:], in_=bb0), inc=("cld", 16))
    s.add("sync", lambda e: e.dma_start(out=preps[:], in_=prep0), inc=("cld", 16))
    lo = emit_load_act(s, ACT, OT, tag="ot")
    s.barrier()

    p1 = emit_proj_spill(s, bufs, [{"w": Wo, "oc": 32}], ACT, "wo", aspill,
                         stats="meanvar", gate=[(lo["ld"], lo["full"]), ("init", 2)])
    f1 = emit_stats_finalize(s, bufs, "f1", wait=(p1["smm"], p1["smm_full"]))
    s.barrier()
    if upto <= 1:
        s.add("gpsimd", lambda e: e.nop(), waits=[(p1["so"], p1["so_full"])])
        s.realize(nc, es, block); es.close(); lower_extended_insts(nc); return nc
    p2 = emit_combine(s, bufs, "c1", [(aspill, bufs["zf"]), (x0T, bufs["xs"])],
                      "postln", x1T, stats="meanvar", gate=[(p1["so"], p1["so_full"])], stats_gate=(f1["fin"], f1["full"]))
    f2 = emit_stats_finalize(s, bufs, "f1", wait=(p2["smm"], p2["smm_full"]))
    s.barrier()
    if upto <= 2:
        s.add("gpsimd", lambda e: e.nop(), waits=[(p2["so"], p2["so_full"])])
        s.realize(nc, es, block); es.close(); lower_extended_insts(nc); return nc
    a1 = emit_apply(s, bufs, x1T, ACT, "a1", gate=[(p2["so"], p2["so_full"])])
    s.barrier()
    if upto <= 3:
        s.add("gpsimd", lambda e: e.nop(), waits=[(p2["so"], p2["so_full"])])
        s.realize(nc, es, block); es.close(); lower_extended_insts(nc); return nc
    p4 = emit_m_path(s, bufs, Aall0, Bflat0, preps, ACT, mspill, "wo",
                     gate=[(a1["apd"], a1["full"]), ("cld", 32)])
    for t in range(2):
        s.add("vector", lambda e, t=t: e.tensor_copy(nm2[t][:], bufs["pSxx"][t][:]),
              waits=[(p4["smm"], p4["smm_full"])], inc=("nmc", 1))
    s.barrier()
    if upto <= 4:
        s.add("gpsimd", lambda e: e.nop(), waits=[(p4["so"], p4["so_full"])])
        s.realize(nc, es, block); es.close(); lower_extended_insts(nc); return nc
    p5 = emit_proj_spill(s, bufs, [{"w": WT0, "oc": 32, "bias": lambda j: bbs[:, j:j + 1]}],
                         ACT, "zp", zspill, stats="norm",
                         gate=[(a1["apd"], a1["full"]), ("nmc", 2)], stats_gate=("nmc", 2))
    if upto <= 5:
        s.add("gpsimd", lambda e: e.nop(), waits=[(p5["so"], p5["so_full"])])
        s.realize(nc, es, block); es.close(); lower_extended_insts(nc); return nc
    s.barrier()
    if upto <= 56:
        s.add("gpsimd", lambda e: e.nop(), waits=[(p5["so"], p5["so_full"])])
        s.realize(nc, es, block); es.close(); lower_extended_insts(nc); return nc
    g = emit_gate(s, bufs, nm2, gam, "gt", parts=gate_parts, wait=(p5["smm"], p5["smm_full"]))
    s.barrier()
    if upto <= 55:
        s.add("gpsimd", lambda e: e.nop(), waits=[(p5["so"], p5["so_full"]), (g["g"], g["full"])])
        s.realize(nc, es, block); es.close(); lower_extended_insts(nc); return nc
    p6 = emit_combine(s, bufs, "c1", [(zspill, bufs["zf"]), (mspill, bufs["xs"])],
                      "gate", fspill, stats="meanvar",
                      gate=[(p5["so"], p5["so_full"]), (p4["so"], p4["so_full"])],
                      gam=gam, extra_vec_wait=(g["g"], g["full"]), stats_gate=(g["g"], g["full"]))
    f3 = emit_stats_finalize(s, bufs, "f1", wait=(p6["smm"], p6["smm_full"]))
    s.barrier()
    if upto <= 6:
        s.add("gpsimd", lambda e: e.nop(), waits=[(p6["so"], p6["so_full"])])
        s.realize(nc, es, block); es.close(); lower_extended_insts(nc); return nc
    p7 = emit_combine(s, bufs, "c1", [(fspill, bufs["zf"]), (x1T, bufs["xs"])],
                      "postln", x2T, stats="meanvar", gate=[(p6["so"], p6["so_full"])], stats_gate=(f3["fin"], f3["full"]))
    f4 = emit_stats_finalize(s, bufs, "f1", wait=(p7["smm"], p7["smm_full"]))
    s.barrier()
    if upto <= 7:
        s.add("gpsimd", lambda e: e.nop(), waits=[(p7["so"], p7["so_full"])])
        s.realize(nc, es, block); es.close(); lower_extended_insts(nc); return nc
    a2 = emit_apply(s, bufs, x2T, ACT, "a1", gate=[(p7["so"], p7["so_full"])])
    s.barrier()
    if upto <= 8:
        s.add("gpsimd", lambda e: e.nop(), waits=[(p7["so"], p7["so_full"])])
        s.realize(nc, es, block); es.close(); lower_extended_insts(nc); return nc
    specs = [
        {"w": Wqkv[:, 0:4096], "oc": 32, "scale": QSCALE, "dst": lambda j: QT1[128 * j:128 * (j + 1), :]},
        {"w": Wqkv[:, 4096:4224], "oc": 1, "scale": 1.0, "dst": lambda j: kT1[0:128, :]},
        {"w": Wqkv[:, 4224:4352], "oc": 1, "scale": 1.0, "dst": lambda j: vT1[0:128, :]},
    ]
    p9 = emit_proj_multi(s, bufs, specs, ACT, tag="qk", act_ready=(a2["apd"], a2["full"]))
    s.add("gpsimd", lambda e: e.nop(),
          waits=[(p9["so"], p9["so_full"]), (p7["so"], p7["so_full"])])
    s.realize(nc, es, block)
    es.close()
    lower_extended_insts(nc)
    return nc



def build_launchD():
    nc = bass.Bass("TRN2", target_bir_lowering=False)
    QTg = nc.dram_tensor("QTg", [1024, S], BF16, kind="ExternalInput").ap()
    kT = nc.dram_tensor("kT", [128, S], BF16, kind="ExternalInput").ap()
    vL = nc.dram_tensor("vL", [S, 128], BF16, kind="ExternalInput").ap()
    slop = nc.dram_tensor("slop", [128, 8], F32, kind="ExternalInput").ap()
    OTg = nc.dram_tensor("OTg", [1024, S], BF16, kind="ExternalOutput").ap()

    es = ExitStack()
    sb, ps = _sb(nc, es), _ps(nc, es)
    KTs = sb("KTs", [128, S], BF16)
    Vs = sb("Vs", [128, S], BF16)
    ones128 = sb("ones128d", [128, 128], BF16)
    Tt = sb("Tt", [128, 512])
    slops = sb("slops", [128, 8])
    qt = [sb(f"qt{i}", [128, 512], BF16) for i in range(2)]
    s1f = [sb(f"s1f{i}", [128, 512]) for i in range(2)]
    Eb = [sb(f"Ebd{i}", [128, 512], BF16) for i in range(2)]
    bvv = [sb(f"bvv{i}", [128, 1]) for i in range(2)]
    rs = sb("rs", [128, 512])
    ost = [sb(f"ostd{i}", [128, 512], BF16) for i in range(2)]
    pS = [ps(f"pS{i}", [128, 512]) for i in range(2)]
    pA = ps("pA", [128, 512])
    pR = ps("pR", [128, 512])

    s = Sched()
    block = es.enter_context(nc.Block())

    s.add("gpsimd", lambda e: e.memset(ones128[:], 1.0), inc=("init", 1))
    s.add("gpsimd", lambda e: e.iota(Tt[:], pattern=[[-1, 512]], base=0,
                                     channel_multiplier=1,
                                     allow_small_or_imprecise_dtypes=True), inc=("init", 1))
    s.add("sync", lambda e: e.dma_start(out=KTs[:], in_=kT), inc=("ld", 16))
    s.add("sync", lambda e: e.dma_start(
        out=Vs[:].rearrange("p (kc d) -> p kc d", kc=32),
        in_=vL.rearrange("(kc p) d -> p kc d", kc=32)), inc=("ld", 16))
    s.add("sync", lambda e: e.dma_start(out=slops[:], in_=slop), inc=("ld", 16))

    for h in range(8):
        for qb in range(8):
            kcs = list(range(0, 4 * qb + 4))
            wq = [("ld", 48)]
            if s.c("qld") >= 32:
                wq.append(("smm", s.c("smm")))
            s.add("sync", lambda e, h=h, qb=qb: e.dma_start(
                out=qt[qb % 2][:], in_=QTg[128 * h:128 * (h + 1), 512 * qb:512 * (qb + 1)]),
                waits=wq, inc=("qld", 16))
            qld_now = s.c("qld")

            def emit_S(idx, kc, h=h, qb=qb, qld_now=qld_now):
                sbk = idx % 2
                w = [("qld", qld_now), ("init", 2)]
                if s.c("smm") >= 2:
                    w.append(("expd", s.c("expd") - 1))
                s.add("tensor", lambda e, kc=kc, sbk=sbk, qb=qb: e.matmul(
                    pS[sbk][:], KTs[:, 128 * kc:128 * (kc + 1)], qt[qb % 2][:],
                    start=True, stop=True), waits=w, inc=("smm", 1))
                wv = [("smm", s.c("smm"))]
                if s.c("stt") >= 2:
                    wv.append(("expd", s.c("expd")))
                s.add("vector", lambda e, sbk=sbk, h=h: e.scalar_tensor_tensor(
                    s1f[sbk][:], Tt[:], slops[:, h:h + 1], pS[sbk][:],
                    op0=ALU.mult, op1=ALU.add), waits=wv, inc=("stt", 1))
                wb = []
                if s.c("bvd") >= 2:
                    wb.append(("expd", s.c("expd")))
                s.add("vector", lambda e, sbk=sbk, kc=kc, qb=qb, h=h: e.tensor_scalar_mul(
                    bvv[sbk][:], slops[:, h:h + 1], float(128 * kc - 512 * qb)),
                    waits=wb, inc=("bvd", 1))
                we = [("stt", s.c("stt")), ("bvd", s.c("bvd"))]
                if s.c("avm") > 0:
                    we.append(("avm", s.c("avm")))
                s.add("scalar", lambda e, sbk=sbk: e.activation(
                    Eb[sbk][:], s1f[sbk][:], AF.Exp, bias=bvv[sbk][:]), waits=we, inc=("expd", 1))
                if kc >= 4 * qb:
                    s.add("gpsimd", lambda e, sbk=sbk, kc=kc, qb=qb: e.affine_select(
                        out=Eb[sbk][:], in_=Eb[sbk][:], compare_op=ALU.is_ge, fill=0.0,
                        base=512 * qb - 128 * kc, channel_multiplier=-1,
                        pattern=[[1, 512]]), waits=[("expd", s.c("expd"))], inc=("affd", 1))
                    return ("affd", s.c("affd"))
                return ("expd", s.c("expd"))

            def emit_AV(idx, kc, ready, first, last, h=h, qb=qb):
                sbk = idx % 2
                wav = [ready]
                if first and s.c("nrm") > 0:
                    wav.append(("nrm", s.c("nrm")))
                s.add("tensor", lambda e, kc=kc, sbk=sbk: e.matmul(
                    pA[:], Vs[:, 128 * kc:128 * (kc + 1)], Eb[sbk][:],
                    start=first, stop=False), waits=wav, inc=("avm", 1))
                s.add("tensor", lambda e, sbk=sbk: e.matmul(
                    pR[:], ones128[:], Eb[sbk][:],
                    start=first, stop=last), inc=("avm", 1))

            prev = None
            for idx, kc in enumerate(kcs):
                ready = emit_S(idx, kc)
                if prev is not None:
                    emit_AV(prev[0], prev[1], prev[2], prev[0] == 0, False)
                prev = (idx, kc, ready)
            emit_AV(prev[0], prev[1], prev[2], prev[0] == 0, True)
            avm_done = s.c("avm")
            wn = [("avm", avm_done)]
            if s.c("oso") >= 32:
                wn.append(("oso", s.c("oso") - 16))
            s.add("vector", lambda e: e.reciprocal(rs[:], pR[:]), waits=wn)
            s.add("vector", lambda e, h=h, qb=qb: e.tensor_mul(
                ost[(8 * h + qb) % 2][:], pA[:], rs[:]), inc=("nrm", 1))
            s.add("gpsimd", lambda e, h=h, qb=qb: e.dma_start(
                out=OTg[128 * h:128 * (h + 1), 512 * qb:512 * (qb + 1)],
                in_=ost[(8 * h + qb) % 2][:]), waits=[("nrm", s.c("nrm"))], inc=("oso", 16))

    s.add("gpsimd", lambda e: e.nop(), waits=[("oso", s.c("oso"))])
    s.realize(nc, es, block)
    es.close()
    lower_extended_insts(nc)
    return nc



def build_launchE(upto=10000):
    nc = bass.Bass("TRN2", target_bir_lowering=False)
    x2T = nc.dram_tensor("x2T", [D, TLOC], F32, kind="ExternalInput").ap()
    OT1 = nc.dram_tensor("OT1", [D, TLOC], BF16, kind="ExternalInput").ap()
    Wc = nc.dram_tensor("Wc", [D, D], F32, kind="ExternalInput").ap()
    WTb = nc.dram_tensor("WTb", [D, D], F32, kind="ExternalInput").ap()
    bbb = nc.dram_tensor("bbb", [128, NCH], F32, kind="ExternalInput").ap()
    WrT = nc.dram_tensor("WrT", [D, 8], F32, kind="ExternalInput").ap()
    brt = nc.dram_tensor("brt", [8, 1], F32, kind="ExternalInput").ap()
    AallL = nc.dram_tensor("AallL", [D, 128], F32, kind="ExternalInput").ap()
    BflatL = nc.dram_tensor("BflatL", [128, D], F32, kind="ExternalInput").ap()
    id8 = nc.dram_tensor("id8", [8, 8], F32, kind="ExternalInput").ap()
    id128 = nc.dram_tensor("id128", [128, 128], F32, kind="ExternalInput").ap()
    xoutT = nc.dram_tensor("xoutT", [D, TLOC], F32, kind="ExternalOutput").ap()
    cspill = nc.dram_tensor("cspill", [D, TLOC], F32, kind="Internal").ap()
    x3T = nc.dram_tensor("x3T", [D, TLOC], F32, kind="Internal").ap()
    eospill = nc.dram_tensor("eospill", [D, TLOC], F32, kind="Internal").ap()
    basespill = nc.dram_tensor("basespill", [D, TLOC], F32, kind="Internal").ap()
    f1spill = nc.dram_tensor("f1spill", [D, TLOC], F32, kind="Internal").ap()

    es = ExitStack()
    bufs = _mk_bufsC(nc, es)
    sb = _sb(nc, es)
    ACT = sb("ACT", [128, NCH * TLOC], BF16)
    bufs["ub"] = pv(49152, 2048, BF16, 1024)
    bbs = sb("bbs", [128, NCH])
    brs = sb("brs", [8, 1])
    id8s = sb("id8s", [8, 8])
    id128s = sb("id128s", [128, 128])
    lgf = sb("lgf", [8, TLOC])
    LGT = sb("LGT", [128, 64])
    SRT = sb("SRT", [128, 64])
    TAU = sb("TAU", [128, 64])
    tauc = sb("tauc", [128, 8])
    WTOK = sb("WTOK", [128, 64])
    W16tok = [sb(f"W16tok{i}", [128, 128]) for i in range(2)]
    uf = sb("uf", [128, TLOC])
    W16T = sb("W16T", [128, TLOC])
    ub2 = sb("ub2", [128, TLOC], BF16)

    s = Sched()
    block = es.enter_context(nc.Block())
    s.add("gpsimd", lambda e: e.memset(bufs["epsc"][:], LN_EPS), inc=("init", 1))
    s.add("gpsimd", lambda e: e.memset(bufs["ones128"][:], 1.0), inc=("init", 1))
    for nm, dst, src_ in (("bbs", bbs, bbb), ("brs", brs, brt), ("id8s", id8s, id8),
                          ("id128s", id128s, id128)):
        s.add("sync", lambda e, dst=dst, src_=src_: e.dma_start(out=dst[:], in_=src_),
              inc=("cld", 16))
    cld_full = s.c("cld")
    lo = emit_load_act(s, ACT, OT1, tag="ot")
    s.barrier()

    p1 = emit_proj_spill(s, bufs, [{"w": Wc, "oc": 32}], ACT, "wc", cspill,
                         stats="meanvar", gate=[(lo["ld"], lo["full"]), ("init", 2)])
    f1 = emit_stats_finalize(s, bufs, "f1", wait=(p1["smm"], p1["smm_full"]))
    s.barrier()
    p2 = emit_combine(s, bufs, "c1", [(cspill, bufs["zf"]), (x2T, bufs["xs"])],
                      "postln", x3T, stats="meanvar", gate=[(p1["so"], p1["so_full"])],
                      stats_gate=(f1["fin"], f1["full"]))
    f2 = emit_stats_finalize(s, bufs, "f1", wait=(p2["smm"], p2["smm_full"]))
    s.barrier()
    a1 = emit_apply(s, bufs, x3T, ACT, "a1", gate=[(p2["so"], p2["so_full"])])
    s.barrier()
    if upto <= 3:
        s.add("gpsimd", lambda e: e.nop(), waits=[(p2["so"], p2["so_full"])])
        s.realize(nc, es, block); es.close(); lower_extended_insts(nc); return nc

    pP, pSx = bufs["pP"], bufs["pSx"]
    wst, wbf = bufs["wst"], bufs["wbf"]
    s.add("sync", lambda e: e.dma_start(
        out=wst[0][:, 0:NCH * 8].rearrange("p (c o) -> p c o", c=NCH),
        in_=WrT.rearrange("(c p) o -> p c o", c=NCH)), waits=[(a1["apd"], a1["full"])], inc=("rlw", 16))
    s.add("gpsimd", lambda e: e.tensor_copy(wbf[0][:, 0:NCH * 8], wst[0][:, 0:NCH * 8]),
          waits=[("rlw", 16)], inc=("rcw", 1))
    s.add("sync", lambda e: e.dma_start(
        out=wst[1][:].rearrange("p (c o) -> p c o", c=NCH),
        in_=AallL.rearrange("(c p) o -> p c o", c=NCH)), inc=("rlw", 16))
    s.add("gpsimd", lambda e: e.tensor_copy(wbf[1][:], wst[1][:]),
          waits=[("rlw", 32)], inc=("rcw", 1))
    for tb in range(2):
        for dc in range(NCH):
            s.add("tensor", lambda e, tb=tb, dc=dc: e.matmul(
                pP[tb][0:8, :], wbf[0][:, 8 * dc:8 * (dc + 1)],
                ACT[:, TLOC * dc + 512 * tb: TLOC * dc + 512 * (tb + 1)],
                start=(dc == 0), stop=(dc == NCH - 1)),
                waits=[("rcw", 1), (a1["apd"], a1["full"])] if dc == 0 else [],
                inc=(("rpm", 1) if dc == NCH - 1 else None))
        s.add("vector", lambda e, tb=tb: e.tensor_scalar_add(
            lgf[:, 512 * tb:512 * (tb + 1)], pP[tb][0:8, :], brs[:, 0:1]),
            waits=[("rpm", s.c("rpm")), ("cld", cld_full)], inc=("rle", 1))
        for dc in range(NCH):
            s.add("tensor", lambda e, tb=tb, dc=dc: e.matmul(
                pSx[tb][:], wbf[1][:, 128 * dc:128 * (dc + 1)],
                ACT[:, TLOC * dc + 512 * tb: TLOC * dc + 512 * (tb + 1)],
                start=(dc == 0), stop=(dc == NCH - 1)),
                waits=[("rcw", 2)] if dc == 0 else [],
                inc=(("upm", 1) if dc == NCH - 1 else None))
        s.add("scalar", lambda e, tb=tb: e.activation(
            uf[:, 512 * tb:512 * (tb + 1)], pSx[tb][:], AF.Copy),
            waits=[("upm", s.c("upm"))], inc=("ue", 1))
    if upto <= 41:
        s.add("gpsimd", lambda e: e.nop(), waits=[("rle", s.c("rle")), ("ue", s.c("ue"))])
        s.realize(nc, es, block); es.close(); lower_extended_insts(nc); return nc
    for tc in range(8):
        s.add("tensor", lambda e, tc=tc: e.transpose(
            pP[2][0:128, 0:8], lgf[0:8, 128 * tc:128 * (tc + 1)], id8s[:]),
            waits=[("rle", 2), ("cld", cld_full)] if tc == 0 else [("lgev", s.c("lgev"))],
            inc=("lgtp", 1))
        s.add("vector", lambda e, tc=tc: e.tensor_copy(LGT[:, 8 * tc:8 * (tc + 1)], pP[2][0:128, 0:8]),
              waits=[("lgtp", s.c("lgtp"))], inc=("lgev", 1))
    if upto <= 42:
        s.add("gpsimd", lambda e: e.nop(), waits=[("lgev", s.c("lgev"))])
        s.realize(nc, es, block); es.close(); lower_extended_insts(nc); return nc
    CEs = [(0,2),(1,3),(4,6),(5,7),(0,4),(1,5),(2,6),(3,7),(0,1),(2,3),(4,5),(6,7),
           (2,4),(3,5),(1,2),(3,4),(5,6),(1,2),(3,4)]
    CEs = [(0,1),(2,3),(4,5),(6,7),(0,2),(1,3),(4,6),(5,7),(1,2),(5,6),(0,4),(3,7),
           (1,5),(2,6),(1,4),(3,6),(2,4),(3,5),(3,4)]
    s.add("vector", lambda e: e.tensor_copy(SRT[:], LGT[:]), waits=[("lgev", s.c("lgev"))],
          inc=("srt", 1))
    for (i, j) in CEs:
        vi = SRT[:].rearrange("p (g k) -> p k g", k=8)[:, i, :]
        vj = SRT[:].rearrange("p (g k) -> p k g", k=8)[:, j, :]
        tt = TAU[:].rearrange("p (g k) -> p k g", k=8)[:, 0, :]
        s.add("vector", lambda e, vi=vi, vj=vj, tt=tt: e.tensor_tensor(tt, vi, vj, op=ALU.min))
        s.add("vector", lambda e, vi=vi, vj=vj: e.tensor_max(vi, vi, vj))
        s.add("vector", lambda e, vj=vj, tt=tt: e.tensor_copy(vj, tt))
    s.add("vector", lambda e: e.nop(), inc=("srt", 1))
    for k in range(1, 8):
        vi = SRT[:].rearrange("p (g k) -> p k g", k=8)[:, k, :]
        vp = SRT[:].rearrange("p (g k) -> p k g", k=8)[:, k - 1, :]
        s.add("vector", lambda e, vi=vi, vp=vp: e.tensor_add(vi, vi, vp))
    for k in range(8):
        vi = SRT[:].rearrange("p (g k) -> p k g", k=8)[:, k, :]
        ti = TAU[:].rearrange("p (g k) -> p k g", k=8)[:, k, :]
        s.add("vector", lambda e, vi=vi, ti=ti, k=k: e.tensor_scalar(
            ti, vi, -1.0, 1.0 / (k + 1), op0=ALU.add, op1=ALU.mult),
            inc=(("tau", 1) if k == 7 else None))
    for g in range(8):
        s.add("vector", lambda e, g=g: e.reduce_max(
            tauc[:, g:g + 1], TAU[:, 8 * g:8 * (g + 1)], axis=AX.X),
            waits=[("tau", 1)], inc=("taur", 1))
    for g in range(8):
        s.add("vector", lambda e, g=g: e.tensor_scalar(
            WTOK[:, 8 * g:8 * (g + 1)], LGT[:, 8 * g:8 * (g + 1)], tauc[:, g:g + 1], 0.0,
            op0=ALU.subtract, op1=ALU.max), waits=[("taur", s.c("taur"))],
            inc=(("wtok", 1) if g == 7 else None))
    if upto <= 43:
        s.add("gpsimd", lambda e: e.nop(), waits=[("wtok", 1)])
        s.realize(nc, es, block); es.close(); lower_extended_insts(nc); return nc
    for tc in range(8):
        dbi = tc % 2
        wcp = [("wtok", 1)]
        if tc >= 2:
            wcp.append(("wttp", s.c("wttp") - 1))
        for r in range(16):
            s.add("vector", lambda e, tc=tc, r=r, dbi=dbi: e.tensor_copy(
                W16tok[dbi][:].rearrange("p (e r) -> p r e", r=16)[:, r, :],
                WTOK[:, 8 * tc:8 * (tc + 1)]),
                waits=wcp if r == 0 else [], inc=(("w16c", 1) if r == 15 else None))
        s.add("tensor", lambda e, tc=tc, dbi=dbi: e.transpose(
            pP[3][:, 0:128], W16tok[dbi][:], id128s[:]),
            waits=[("w16c", s.c("w16c")), ("cld", cld_full)] + ([("repe", s.c("repe"))] if tc >= 1 else []),
            inc=("wttp", 1))
        s.add("vector", lambda e, tc=tc: e.tensor_copy(
            W16T[:, 128 * tc:128 * (tc + 1)], pP[3][:, 0:128]),
            waits=[("wttp", s.c("wttp"))], inc=("repe", 1))
    if upto <= 445:
        s.add("gpsimd", lambda e: e.nop(), waits=[("repe", s.c("repe"))])
        s.realize(nc, es, block); es.close(); lower_extended_insts(nc); return nc
    for tb in range(2):
        s.add("vector", lambda e, tb=tb: e.tensor_mul(
            ub2[:, 512 * tb:512 * (tb + 1)], uf[:, 512 * tb:512 * (tb + 1)],
            W16T[:, 512 * tb:512 * (tb + 1)]),
            waits=[("repe", 8), ("ue", 2)], inc=("ub2d", 1))
    if upto <= 45:
        s.add("gpsimd", lambda e: e.nop(), waits=[("ub2d", s.c("ub2d"))])
        s.realize(nc, es, block); es.close(); lower_extended_insts(nc); return nc
    s.barrier()

    s.add("sync", lambda e: e.dma_start(out=wst[1][:], in_=BflatL), inc=("blw", 16))
    s.add("gpsimd", lambda e: e.tensor_copy(wbf[1][:], wst[1][:]), waits=[("blw", 16)], inc=("bcw", 1))
    zf = bufs["zf"]
    for ow in range(NCH):
        for tb in range(2):
            bank = (ow % 2) * 2 + tb
            wt = [("bcw", 1), ("ub2d", 2), ("rle", 2)]
            if ow >= 2:
                wt.append(("bpe", 2 * (ow - 2) + tb + 1))
            s.add("tensor", lambda e, ow=ow, tb=tb, bank=bank: e.matmul(
                pP[bank][:], wbf[1][:, 128 * ow:128 * (ow + 1)],
                ub2[:, 512 * tb:512 * (tb + 1)], start=True, stop=True),
                waits=wt, inc=("bpm", 1))
            ev_w = [("bpm", s.c("bpm"))]
            if ow >= 2 and tb == 0:
                ev_w.append(("bso", 16 * (ow - 1)))
            s.add("scalar", lambda e, ow=ow, tb=tb, bank=bank: e.activation(
                zf[ow % 2][:, 512 * tb:512 * (tb + 1)], pP[bank][:], AF.Copy),
                waits=ev_w, inc=("bpe", 1))
        s.add("gpsimd", lambda e, ow=ow: e.dma_start(
            out=eospill[128 * ow:128 * (ow + 1), :], in_=zf[ow % 2][:]),
            waits=[("bpe", s.c("bpe"))], inc=("bso", 16))
    s.barrier()

    p6 = emit_proj_spill(s, bufs, [{"w": WTb, "oc": 32, "bias": lambda j: bbs[:, j:j + 1]}],
                         ACT, "bs", basespill, stats="none",
                         gate=[(a1["apd"], a1["full"]), ("cld", cld_full)])
    s.barrier()
    p7 = emit_combine(s, bufs, "c2", [(basespill, bufs["zf"]), (eospill, bufs["xs"])],
                      "add", f1spill, stats="meanvar",
                      gate=[(p6["so"], p6["so_full"]), ("bso", s.c("bso"))],
                      stats_gate=("ub2d", 2))
    f3 = emit_stats_finalize(s, bufs, "f1", wait=(p7["smm"], p7["smm_full"]))
    s.barrier()
    p8 = emit_combine(s, bufs, "c3", [(f1spill, bufs["zf"]), (x3T, bufs["xs"])],
                      "postln", xoutT, stats="none", gate=[(p7["so"], p7["so_full"])])
    s.add("gpsimd", lambda e: e.nop(), waits=[(p8["so"], p8["so_full"])])
    s.realize(nc, es, block)
    es.close()
    lower_extended_insts(nc)
    return nc



NS_GRP = (32, 16)
NS1_GRP = (16, 32)


def _kv_fetch_seq(blk):
    RG = NS_GRP if blk == 0 else NS1_GRP
    ngroups = 4 if blk == 0 else 1
    for Ggrp in range(2):
        R = RG[Ggrp]
        for g in range(ngroups):
            yield ("K", Ggrp, g, 0)
            for half in range(R // 16):
                yield ("V", Ggrp, g, half)


def _sched_tables(core):
    r = core % 4
    skF = []
    for (kind, Ggrp, g, half) in _kv_fetch_seq(0):
        imin = 4 * Ggrp
        kc0 = r + 4 * imin
        if kind == "K":
            skF.append((128 * g) * 4480 + 128 * kc0)
        else:
            skF.append(128 * (kc0 + 16 * half) * 512 + 128 * g)
    for (kind, Ggrp, g, half) in _kv_fetch_seq(1):
        imax = 4 * Ggrp + 3
        R = NS1_GRP[Ggrp]
        kcmin = r + 4 * imax - R + 1
        if kind == "K":
            skF.append(384 + 128 * kcmin)
        else:
            skF.append((384 + 128 * (kcmin + 16 * half)) * 128)
    corr0 = np.full((1, 1024), -128.0 * r, dtype=np.float32)
    if r == 3:
        corr0[0, 1023] = -(128.0 * r - 4096.0)
    corr1 = np.zeros((1, 32 * 1024), dtype=np.float32)
    pp = np.arange(128, dtype=np.float64)
    tt_ = np.arange(4, dtype=np.float64)
    jj = np.arange(128, dtype=np.float64)
    colbase = (512.0 * (3.0 - tt_)[:, None] - jj[None, :]).reshape(512)
    for h in range(32):
        sl = float(ALIBI_SLOPES[h])
        sp = np.exp(sl * pp).sum()
        for Ggrp in range(2):
            imax = 4 * Ggrp + 3
            pol = np.zeros(512, dtype=np.float64)
            for u in range(NS1_GRP[Ggrp]):
                kc = r + 4 * imax - u
                if kc < 0:
                    pol += sp * np.exp(sl * (colbase - 128.0 * u))
            corr1[0, 1024 * h + 512 * Ggrp: 1024 * h + 512 * (Ggrp + 1)] = -pol
    flag = 1.0 if r == 3 else 0.0
    c1hi = corr1.astype(BF)
    c1lo = (corr1 - c1hi.astype(np.float32)).astype(BF)
    corr1_2 = np.concatenate([c1hi, c1lo], axis=1)
    skFa = np.zeros((1, 64), dtype=np.int32)
    skFa[0, :len(skF)] = skF
    return (skFa, corr0.astype(BF), corr1_2, np.array([[flag]], dtype=BF))


def emit_attn_fused(s, bufs, blk, qspill, KTg, VLg, skF_sb, fetch_base,
                    corr_sb, slop_sb, flag_sb, vsum_sb, ACT, tag, gate):
    kvK, kvV = bufs["kvK"], bufs["kvV"]
    qstg, Eb, s1f, bvv = bufs["qstg"], bufs["Eb"], bufs["s1f"], bufs["bvv"]
    rsn, Tt, ones128, onesrow = bufs["rs"], bufs["Tt"], bufs["ones128"], bufs["onesrow"]
    pS, pA, pR = bufs["pS"], bufs["pA"], bufs["pR"]
    kvd, smm, expd, affd, avm, nrm, qld, sttc, bvd, crr = (
        tag + n for n in ("kv", "smm", "expd", "affd", "avm", "nrm", "qld", "stt", "bvd", "crr"))
    RGRP = NS_GRP if blk == 0 else NS1_GRP
    ngroups = 4 if blk == 0 else 1
    heads_per = 8 if blk == 0 else 32
    fetch_idx = [fetch_base]
    fetch_plan = {}
    for (kind, Gg, gg, half) in _kv_fetch_seq(blk):
        fetch_plan.setdefault((Gg, gg), []).append((kind, half, fetch_idx[0]))
        fetch_idx[0] += 1
    slot_base = 0
    for Ggrp in range(2):
        R = RGRP[Ggrp]
        for g in range(ngroups):
            wk = list(gate) if (Ggrp == 0 and g == 0) else []
            if s.c(smm) > 0:
                wk.append((smm, s.c(smm)))
                wk.append((avm, s.c(avm)))
            for (kind, half, fidx) in fetch_plan[(Ggrp, g)]:
                if kind == "K":
                    def kload(e, fidx=fidx, R=R):
                        reg = bufs.get("_dyng")
                        if reg is None:
                            reg = bufs["_dyng"] = e.alloc_register("dyng")
                        e.reg_load(reg, skF_sb[0:1, fidx:fidx + 1])
                        src = bass.AP(KTg.tensor, reg, [[KTg.shape[1], 128], [1, 128 * R]])
                        return e.dma_start(out=kvK[:, 0:128 * R], in_=src)
                    s.add("gpsimd", kload, waits=wk, inc=(kvd, 16))
                else:
                    def vload(e, fidx=fidx, half=half):
                        reg = bufs.get("_dyns")
                        if reg is None:
                            reg = bufs["_dyns"] = e.alloc_register("dyns")
                        e.reg_load(reg, skF_sb[0:1, fidx:fidx + 1])
                        W = VLg.shape[1]
                        src = bass.AP(VLg.tensor, reg, [[W, 128], [128 * W, 16], [1, 128]])
                        return e.dma_start(
                            out=kvV[:, 2048 * half:2048 * (half + 1)].rearrange(
                                "p (u d) -> p u d", u=16), in_=src)
                    s.add("scalar", vload, waits=wk, inc=(kvd, 16))
                wk = [w for w in wk if w[0] not in (smm, avm)]
            kv_ready = s.c(kvd)
            for hh in range(heads_per):
                h = g * heads_per + hh if blk == 0 else hh
                wq = list(gate) if s.c(qld) == 0 else []
                if s.c(qld) >= 32:
                    wq.append((smm, s.c(smm)))
                qi = (s.c(qld) // 16) % 2
                s.add("sync", lambda e, h=h, Ggrp=Ggrp, qi=qi: e.dma_start(
                    out=qstg[qi][:],
                    in_=qspill[128 * h:128 * (h + 1), 512 * Ggrp:512 * (Ggrp + 1)]),
                    waits=wq, inc=(qld, 16))
                qbuf = qstg[qi]
                qld_now = s.c(qld)
                if blk == 1:
                    wcr = [("cld", 0)] if False else []
                    if s.c(crr) > 0:
                        wcr.append((avm, s.c(avm)))
                    s.add("sync", lambda e, h=h: e.dma_start(
                        out=corr_sb[1][0:1, 0:1024],
                        in_=corr_sb[0][0:1, 1024 * h:1024 * (h + 1)]),
                        waits=wcr, inc=(crr, 16))
                    s.add("sync", lambda e, h=h: e.dma_start(
                        out=corr_sb[1][0:1, 1024:2048],
                        in_=corr_sb[0][0:1, 32768 + 1024 * h:32768 + 1024 * (h + 1)]),
                        inc=(crr, 16))

                def emit_S(idx, u, h=h, Ggrp=Ggrp, qbuf=qbuf, qld_now=qld_now,
                           kv_ready=kv_ready, R=R):
                    sbk = idx % 2
                    w = [(qld, qld_now), (kvd, kv_ready)]
                    if s.c(smm) == 0:
                        w += list(gate)
                    if s.c(smm) >= 2:
                        w.append(((sttc if blk == 1 else expd), s.c(sttc if blk == 1 else expd) - 1))
                    pu = u if blk == 0 else (R - 1 - u)
                    s.add("tensor", lambda e, pu=pu, sbk=sbk, qbuf=qbuf: e.matmul(
                        pS[sbk][:], kvK[:, 128 * pu:128 * (pu + 1)], qbuf[:],
                        start=True, stop=True), waits=w, inc=(smm, 1))
                    if blk == 1:
                        wv = [(smm, s.c(smm))]
                        if s.c(sttc) == 0:
                            wv += [w_ for w_ in gate if w_[0] == "cld"]
                        if s.c(sttc) >= 2:
                            wv.append((expd, s.c(expd)))
                        s.add("vector", lambda e, sbk=sbk, h=h: e.scalar_tensor_tensor(
                            s1f[sbk][:], Tt[:], slop_sb[:, h:h + 1], pS[sbk][:],
                            op0=ALU.mult, op1=ALU.add), waits=wv, inc=(sttc, 1))
                        wb = []
                        if s.c(bvd) >= 2:
                            wb.append((expd, s.c(expd)))
                        s.add("vector", lambda e, sbk=sbk, u=u, h=h: e.tensor_scalar_mul(
                            bvv[sbk][:], slop_sb[:, h:h + 1], float(-128 * u)),
                            waits=wb, inc=(bvd, 1))
                        we = [(sttc, s.c(sttc)), (bvd, s.c(bvd))]
                        if s.c(avm) > 0:
                            we.append((avm, s.c(avm)))
                        s.add("scalar", lambda e, sbk=sbk: e.activation(
                            Eb[sbk][:], s1f[sbk][:], AF.Exp, bias=bvv[sbk][:]),
                            waits=we, inc=(expd, 1))
                    else:
                        we = [(smm, s.c(smm))]
                        if s.c(avm) > 0:
                            we.append((avm, s.c(avm)))
                        s.add("scalar", lambda e, sbk=sbk: e.activation(
                            Eb[sbk][:], pS[sbk][:], AF.Exp), waits=we, inc=(expd, 1))
                    if blk == 0 and u <= 15:
                        s.add("gpsimd", lambda e, sbk=sbk, u=u: e.affine_select(
                            out=Eb[sbk][:].rearrange("p (t j) -> p t j", t=4),
                            in_=Eb[sbk][:].rearrange("p (t j) -> p t j", t=4),
                            compare_op=ALU.is_ge, fill=0.0,
                            base=128 * u - 1, channel_multiplier=1,
                            pattern=[[-512, 4], [-1, 128]]),
                            waits=[(expd, s.c(expd))], inc=(affd, 1))
                        return (affd, s.c(affd))
                    if blk == 1 and u <= 12:
                        s.add("gpsimd", lambda e, sbk=sbk, u=u: e.affine_select(
                            out=Eb[sbk][:].rearrange("p (t j) -> p t j", t=4),
                            in_=Eb[sbk][:].rearrange("p (t j) -> p t j", t=4),
                            compare_op=ALU.is_ge, fill=0.0,
                            base=128 * u - 1536, channel_multiplier=-1,
                            pattern=[[512, 4], [1, 128]]),
                            waits=[(expd, s.c(expd))], inc=(affd, 1))
                        return (affd, s.c(affd))
                    return (expd, s.c(expd))

                def emit_AV(idx, u, ready, first, h=h, R=R):
                    sbk = idx % 2
                    wav = [ready]
                    if first and s.c(nrm) > 0:
                        wav.append((nrm, s.c(nrm)))
                    pu = u if blk == 0 else (R - 1 - u)
                    s.add("tensor", lambda e, pu=pu, sbk=sbk: e.matmul(
                        pA[:], kvV[:, 128 * pu:128 * (pu + 1)], Eb[sbk][:],
                        start=first, stop=False), waits=wav, inc=(avm, 1))
                    s.add("tensor", lambda e, sbk=sbk: e.matmul(
                        pR[:], ones128[:], Eb[sbk][:],
                        start=first, stop=False), inc=(avm, 1))

                prev = None
                for idx in range(R):
                    ready = emit_S(idx, idx)
                    if prev is not None:
                        emit_AV(prev[0], prev[1], prev[2], prev[0] == 0)
                    prev = (idx, idx, ready)
                emit_AV(prev[0], prev[1], prev[2], prev[0] == 0)
                if blk == 0:
                    s.add("tensor", lambda e, Ggrp=Ggrp: e.matmul(
                        pR[:], onesrow[:], corr_sb[0:1, 512 * Ggrp:512 * (Ggrp + 1)],
                        start=False, stop=True), inc=(avm, 1))
                else:
                    wcm = [(crr, s.c(crr))]
                    s.add("tensor", lambda e, Ggrp=Ggrp: e.matmul(
                        pR[:], onesrow[:], corr_sb[1][0:1, 512 * Ggrp:512 * (Ggrp + 1)],
                        start=False, stop=False), waits=wcm, inc=(avm, 1))
                    s.add("tensor", lambda e, Ggrp=Ggrp: e.matmul(
                        pR[:], onesrow[:], corr_sb[1][0:1, 1024 + 512 * Ggrp:1024 + 512 * (Ggrp + 1)],
                        start=False, stop=True), inc=(avm, 1))
                if blk == 0 and Ggrp == 1:
                    s.add("tensor", lambda e, g=g: e.matmul(
                        pA[0:128, 511:512], vsum_sb[0:1, 128 * g:128 * (g + 1)],
                        flag_sb[0:1, 0:1], start=False, stop=True),
                        waits=[], inc=(avm, 1))
                else:
                    s.add("tensor", lambda e: e.matmul(
                        pA[:, 0:1], onesrow[:], bufs["zrow"][0:1, 0:1],
                        start=False, stop=True), inc=(avm, 1))
                avm_done = s.c(avm)
                s.add("vector", lambda e: e.reciprocal(rsn[:], pR[:]),
                      waits=[(avm, avm_done)])
                s.add("vector", lambda e, h=h, Ggrp=Ggrp: e.tensor_mul(
                    ACT[:, TLOC * h + 512 * Ggrp: TLOC * h + 512 * (Ggrp + 1)],
                    pA[:], rsn[:]), inc=(nrm, 1))
        slot_base += (ngroups * R) if blk == 0 else R
    return {"nrm": nrm, "nrm_full": s.c(nrm), "avm": avm, "avm_full": s.c(avm)}


def build_fused(debug=False):
    nc = bass.Bass("TRN2", target_bir_lowering=False)
    dr = lambda n, sh, dt, kind="ExternalInput": nc.dram_tensor(n, sh, dt, kind=kind).ap()
    x0T = dr("x0T", [D, TLOC], F32)
    Wq = dr("Wq", [D, 4096], F32)
    Wk = dr("Wk", [D, 512], F32)
    Wv = dr("Wv", [D, 512], F32)
    Wo = dr("Wo", [D, D], F32)
    WT0 = dr("WT0", [D, D], F32)
    bb0 = dr("bb0", [128, NCH], F32)
    Aall0 = dr("Aall0", [D, 128], F32)
    Bflat0 = dr("Bflat0", [128, D], F32)
    prep0 = dr("prep0", [128, 1], F32)
    Wqkv = dr("Wqkv", [D, 4352], F32)
    Wc = dr("Wc", [D, D], F32)
    WTb = dr("WTb", [D, D], F32)
    bbb = dr("bbb", [128, NCH], F32)
    WrT = dr("WrT", [D, 8], F32)
    brt = dr("brt", [8, 1], F32)
    AallL = dr("AallL", [D, 128], F32)
    BflatL = dr("BflatL", [128, D], F32)
    id8 = dr("id8", [8, 8], F32)
    id128 = dr("id128", [128, 128], F32)
    slopA = dr("slopA", [128, 32], F32)
    corr0d = dr("corr0d", [1, 1024], BF16)
    corr1d = dr("corr1d", [1, 2 * 32 * 1024], BF16)
    flagd = dr("flagd", [1, 1], BF16)
    skFd = dr("skFd", [1, 64], I32)
    xoutT = dr("xoutT", [D, TLOC], F32, kind="ExternalOutput")
    di = lambda n, sh, dt: nc.dram_tensor(n, sh, dt,
                                          kind=("ExternalOutput" if debug else "Internal")).ap()
    x1T = di("x1T", [D, TLOC], F32)
    x2T = di("x2T", [D, TLOC], F32)
    x3T = di("x3T", [D, TLOC], F32)
    aspill = di("aspill", [D, TLOC], F32)
    mspill = di("mspill", [D, TLOC], F32)
    zspill = di("zspill", [D, TLOC], F32)
    fspill = di("fspill", [D, TLOC], F32)
    cspill = di("cspill", [D, TLOC], F32)
    eospill = di("eospill", [D, TLOC], F32)
    basespill = di("basespill", [D, TLOC], F32)
    f1spill = di("f1spill", [D, TLOC], F32)
    qspill0 = di("qspill0", [4096, TLOC], BF16)
    qspill1 = di("qspill1", [4096, TLOC], BF16)
    dii = lambda n, sh, dt: nc.dram_tensor(n, sh, dt, kind="Internal").ap()
    ccK0in = dii("ccK0in", [512, TLOC], BF16)
    ccV0in = dii("ccV0in", [TLOC, 512], BF16)
    ccK0out = dii("ccK0out", [2048, 1024], BF16)
    ccV0out = dii("ccV0out", [4096, 512], BF16)
    ccK1in = dii("ccK1in", [128, TLOC], BF16)
    ccV1in = dii("ccV1in", [TLOC, 128], BF16)
    ccK1out = dii("ccK1out", [512, 1024], BF16)
    ccV1out = dii("ccV1out", [4096, 128], BF16)
    KTg0 = di("KTg0", [512, 4480], BF16)
    VLg0 = di("VLg0", [4480, 512], BF16)
    KTg1 = di("KTg1", [128, 4480], BF16)
    VLg1 = di("VLg1", [4480, 128], BF16)

    es = ExitStack()
    sb, ps = _sb(nc, es), _ps(nc, es)
    POOL = sb("pool", [128, 65536], mybir.dt.uint8)

    def pv(off, nbytes, dtype, cols):
        return POOL[:, off:off + nbytes].bitcast(dtype)

    wst_v = pv(0, 16384, F32, 4096)
    wbf_v = [pv(16384, 8192, BF16, 4096), pv(24576, 8192, BF16, 4096)]
    zf_v = [pv(32768, 4096, F32, 1024), pv(36864, 4096, F32, 1024)]
    zb_v = [pv(40960, 2048, BF16, 1024), pv(43008, 2048, BF16, 1024)]
    ost_v = [pv(45056, 2048, BF16, 1024), pv(47104, 2048, BF16, 1024)]
    res_v = [pv(49152, 4096, F32, 1024), pv(53248, 4096, F32, 1024)]
    kvK_v = pv(0, 8192, BF16, 4096)
    kvV_v = pv(8192, 8192, BF16, 4096)
    qstg_v = [pv(16384, 1024, BF16, 512), pv(17408, 1024, BF16, 512)]
    Eb_v = [pv(18432, 1024, BF16, 512), pv(19456, 1024, BF16, 512)]
    s1f_v = [pv(20480, 2048, F32, 512), pv(22528, 2048, F32, 512)]
    vstg_v = [pv(24576, 1024, BF16, 512), pv(25600, 1024, BF16, 512)]

    bufs = {
        "xs": [sb(f"xs{i}", [128, TLOC]) for i in range(2)],
        "xb": [sb(f"xb{i}", [128, TLOC], BF16) for i in range(2)],
        "sq": [sb(f"sq{i}", [128, TLOC], BF16) for i in range(2)],
        "ones128": sb("ones128", [128, 128], BF16),
        "epsc": sb("epsc", [128, 1]),
        "rt": [sb(f"rt{t}", [128, 512]) for t in range(2)],
        "mrt": [sb(f"mrt{t}", [128, 512]) for t in range(2)],
        "scr": [sb(f"scr{t}", [128, 512]) for t in range(4)],
        "aps": [sb(f"aps{t}", [128, 512]) for t in range(2)],
        "wst": [wst_v, wst_v],
        "wbf": wbf_v,
        "ost": ost_v,
        "zf": zf_v,
        "res": res_v,
        "resb": zb_v,
        "kvK": kvK_v, "kvV": kvV_v, "qstg": qstg_v, "Eb": Eb_v, "s1f": s1f_v,
        "bvv": [sb(f"bvv{i}", [128, 1]) for i in range(2)],
        "rs": sb("rsn", [128, 512]),
        "Tt": sb("Tt", [128, 512]),
        "onesrow": sb("onesrow", [1, 128], BF16),
        "zrow": sb("zrow", [1, 1], BF16),
    }
    pbanks = [ps(f"pb{i}", [128, 512]) for i in range(8)]
    bufs["pP"] = pbanks[0:4]
    bufs["pSx"] = pbanks[4:6]
    bufs["pSxx"] = pbanks[6:8]
    bufs["pS"] = pbanks[0:2]
    bufs["pA"] = pbanks[2]
    bufs["pR"] = pbanks[3]
    pA, pR = pbanks[2], pbanks[3]
    ACT = sb("ACT", [128, NCH * TLOC], BF16)
    bufs["ub"] = pv(49152, 2048, BF16, 1024)
    bbs0 = sb("bbs0", [128, NCH])
    preps = sb("preps", [128, 1])
    bbsE = sb("bbsE", [128, NCH])
    brs = sb("brs", [8, 1])
    id8s = sb("id8s", [8, 8])
    id128s = sb("id128s", [128, 128])
    slops = sb("slops", [128, 32])
    corr0s = sb("corr0s", [1, 1024], BF16)
    corr1s = sb("corr1s", [1, 2048], BF16)
    vsum_sb = sb("vsum", [1, 512], BF16)
    flag_sb = sb("flag", [1, 1], BF16)
    zpad = sb("zpad", [128, 2048], BF16)
    skFs = sb("skFs", [1, 64], I32)
    nm2 = [sb(f"nm2{t}", [128, 512]) for t in range(2)]
    gam = [sb(f"gam{t}", [128, 512]) for t in range(2)]
    LGT = sb("LGT", [128, 64])
    SRT = sb("SRT", [128, 64])
    TAU = sb("TAU", [128, 64])
    tauc = sb("tauc", [128, 8])
    WTOK = sb("WTOK", [128, 64])
    W16tok = [sb(f"W16tok{i}", [128, 128]) for i in range(2)]
    uf = pv(32768, 4096, F32, 1024)
    W16T = pv(36864, 4096, F32, 1024)
    lgf = pv(40960, 4096, F32, 1024)[0:8, :]
    ub2 = pv(45056, 2048, BF16, 1024)

    s = Sched()
    block = es.enter_context(nc.Block())

    s.add("gpsimd", lambda e: e.memset(bufs["epsc"][:], LN_EPS), inc=("init", 1))
    s.add("gpsimd", lambda e: e.memset(bufs["ones128"][:], 1.0), inc=("init", 1))
    s.add("gpsimd", lambda e: e.memset(bufs["onesrow"][:], 1.0), inc=("init", 1))
    s.add("gpsimd", lambda e: e.memset(bufs["zrow"][:], 0.0), inc=("init", 1))
    s.add("gpsimd", lambda e: e.memset(zpad[:], 0.0), inc=("init", 1))
    s.add("gpsimd", lambda e: e.iota(bufs["Tt"][:].rearrange("p (t j) -> p t j", t=4),
                                     pattern=[[-512, 4], [-1, 128]], base=1536,
                                     channel_multiplier=1,
                                     allow_small_or_imprecise_dtypes=True), inc=("init", 1))
    INIT_FULL = s.c("init")
    for dst, src_ in ((bbs0, bb0), (preps, prep0), (bbsE, bbb), (brs, brt),
                      (id8s, id8), (id128s, id128), (slops, slopA),
                      (corr0s, corr0d), (flag_sb, flagd), (skFs, skFd)):
        s.add("sync", lambda e, dst=dst, src_=src_: e.dma_start(out=dst[:], in_=src_),
              inc=("cld", 16))
    CLD_FULL = s.c("cld")
    pad_dsts = [bass.AP(KTg0.tensor, (128 * gq) * 4480 + 4096, [[4480, 128], [1, 384]])
                for gq in range(4)]
    pad_dsts.append(bass.AP(VLg0.tensor, 4096 * 512, [[1536, 128], [1, 1536]]))
    pad_dsts.append(bass.AP(KTg1.tensor, 0, [[4480, 128], [1, 384]]))
    pad_dsts.append(bass.AP(VLg1.tensor, 0, [[384, 128], [1, 384]]))
    for dst in pad_dsts:
        s.add("gpsimd", lambda e, dst=dst: e.dma_start(out=dst, in_=zpad[:, 0:dst.shape[-1]]),
              waits=[("init", INIT_FULL)], inc=("pz", 16))
    PZ_FULL = s.c("pz")

    ln0 = emit_ln(s, bufs, x0T, ACT, tag="a")

    specsA = [
        {"w": Wq, "oc": 32, "scale": QSCALE, "dst": lambda j: qspill0[128 * j:128 * (j + 1), :]},
        {"w": Wk, "oc": 4, "scale": 1.0, "dst": lambda j: ccK0in[128 * j:128 * (j + 1), :]},
    ]
    pj0 = emit_proj_multi(s, bufs, specsA, ACT, tag="p",
                          act_ready=(ln0["apd"], ln0["apd_full"]))
    wvst = bufs["xs"]
    vso_name = "vso"
    for half in range(2):
        for dc in range(NCH):
            w = [("vcw", s.c("vcw") - 1)] if (dc >= 2 or half > 0) else \
                [(pj0["pe"], pj0["pe_full"]), (pj0["so"], pj0["so_full"])]
            s.add("sync", lambda e, dc=dc: e.dma_start(out=wvst[dc % 2][:, 0:512],
                                                       in_=Wv[128 * dc:128 * (dc + 1), :]),
                  waits=w, inc=("vlw", 16))
            wc = [("vlw", s.c("vlw"))]
            if dc >= 2 or half > 0:
                wc.append(("vpm", s.c("vpm") - 4))
            s.add("gpsimd", lambda e, dc=dc: e.tensor_copy(bufs["sq"][dc % 2][:, 0:512],
                                                           wvst[dc % 2][:, 0:512]),
                  waits=wc, inc=("vcw", 1))
            for tq in range(4):
                tc = 4 * half + tq
                wt = [("vcw", s.c("vcw"))]
                if dc == 0 and half == 0:
                    wt += [(pj0["pe"], pj0["pe_full"]), (ln0["apd"], ln0["apd_full"])]
                if dc == 0 and half == 1:
                    wt += [("vpe", s.c("vpe"))]
                s.add("tensor", lambda e, dc=dc, tc=tc, tq=tq: e.matmul(
                    pbanks[tq][:], ACT[:, TLOC * dc + 128 * tc: TLOC * dc + 128 * (tc + 1)],
                    bufs["sq"][dc % 2][:, 0:512], start=(dc == 0), stop=(dc == NCH - 1)),
                    waits=wt, inc=("vpm", 1))
        for tq in range(4):
            tc = 4 * half + tq
            s.add("scalar", lambda e, tq=tq, tc=tc: e.activation(
                ost_v[tq % 2][:, 0:512], pbanks[tq][:], AF.Copy),
                waits=[("vpm", s.c("vpm"))] + ([("vso", s.c("vso") - 16)] if s.c("vso") >= 32 else []),
                inc=("vpe", 1))
            s.add("gpsimd", lambda e, tc=tc, tq=tq: e.dma_start(
                out=ccV0in[128 * tc:128 * (tc + 1), :], in_=ost_v[tq % 2][:, 0:512]),
                waits=[("vpe", s.c("vpe"))], inc=("vso", 16))
    VSO_FULL = s.c("vso")

    s.add("gpsimd", lambda e: e.collective_compute(
        "AllGather", ALU.bypass, replica_groups=[[0, 1, 2, 3], [4, 5, 6, 7]],
        ins=[ccK0in], outs=[ccK0out]),
        waits=[(pj0["so"], pj0["so_full"]), ("pz", PZ_FULL)], inc=("cc0", 1))
    s.add("gpsimd", lambda e: e.collective_compute(
        "AllGather", ALU.bypass, replica_groups=[[0, 1, 2, 3], [4, 5, 6, 7]],
        ins=[ccV0in], outs=[ccV0out]),
        waits=[("vso", VSO_FULL)], inc=("cc0", 1))
    for rho in range(4):
        for gq in range(4):
            s.add("sync", lambda e, rho=rho, gq=gq: e.dma_start(
                out=bass.AP(KTg0.tensor, (128 * gq + 512 * 0) * 4480 * 0 + (128 * gq) * 4480 + 128 * rho,
                            [[4480, 128], [512, 8], [1, 128]]),
                in_=bass.AP(ccK0out.tensor, (512 * rho + 128 * gq) * 1024,
                            [[1024, 128], [128, 8], [1, 128]])),
                waits=[("cc0", 2), ("pz", PZ_FULL)], inc=("rr0", 16))
        s.add("sync", lambda e, rho=rho: e.dma_start(
            out=bass.AP(VLg0.tensor, (128 * rho) * 512,
                        [[262144, 8], [512, 128], [1, 512]]),
            in_=bass.AP(ccV0out.tensor, (1024 * rho) * 512,
                        [[65536, 8], [512, 128], [1, 512]])),
            waits=[("cc0", 2), ("pz", PZ_FULL)], inc=("rr0", 16))
    RR0_FULL = s.c("rr0")
    for m in range(32):
        w = [("cc0", 2), ("afin", s.c("afin")), ("init", INIT_FULL)]
        if m >= 2:
            w = [("vsl", s.c("vsl") - 16)]
        s.add("sync", lambda e, m=m: e.dma_start(out=vstg_v[m % 2][:],
                                                 in_=ccV0out[128 * m:128 * (m + 1), :]),
              waits=w, inc=("vsl", 16))
        s.add("tensor", lambda e, m=m: e.matmul(
            pbanks[4][0:1, :], bufs["ones128"][:, 0:1], vstg_v[m % 2][:],
            start=(m == 0), stop=(m == 31)),
            waits=[("vsl", s.c("vsl"))], inc=("vsm", 1))
    s.add("scalar", lambda e: e.activation(vsum_sb[:], pbanks[4][0:1, :], AF.Copy),
          waits=[("vsm", 32)], inc=("vse", 1))

    s.full_fence()
    at0 = emit_attn_fused(s, bufs, 0, qspill0, KTg0, VLg0, skFs, 0,
                          corr0s, slops, flag_sb, vsum_sb, ACT, "x",
                          gate=[("rr0", RR0_FULL), ("cld", CLD_FULL), ("vse", 1),
                                (pj0["pe"], pj0["pe_full"]), ("vpe", s.c("vpe")),
                                (pj0["so"], pj0["so_full"])])

    if debug:
        for nmk, srck, gatek in (("dbgK0", ccK0in, (pj0["so"], pj0["so_full"])),
                                 ("dbgV0", ccV0in, ("vso", VSO_FULL))):
            dap = nc.dram_tensor(nmk, list(srck.shape), BF16, kind="ExternalOutput").ap()
            s.add("gpsimd", lambda e, dap=dap, srck=srck: e.dma_start(out=dap, in_=srck),
                  waits=[gatek], inc=("dbg", 16))
        dbgA0 = nc.dram_tensor("dbgA0", [D, TLOC], BF16, kind="ExternalOutput").ap()
        for c in range(NCH):
            s.add("gpsimd", lambda e, c=c: e.dma_start(
                out=dbgA0[128 * c:128 * (c + 1), :], in_=ACT[:, TLOC * c:TLOC * (c + 1)]),
                waits=[(at0["nrm"], at0["nrm_full"])], inc=("dbg", 16))
    p1 = emit_proj_spill(s, bufs, [{"w": Wo, "oc": 32}], ACT, "wo", aspill,
                         stats="meanvar",
                         gate=[(at0["nrm"], at0["nrm_full"])],
                         stats_gate=("vse", 1))
    f1 = emit_stats_finalize(s, bufs, "f1", wait=(p1["smm"], p1["smm_full"]))
    s.barrier()
    p2 = emit_combine(s, bufs, "c1", [(aspill, bufs["zf"]), (x0T, bufs["xs"])],
                      "postln", x1T, stats="meanvar", gate=[(p1["so"], p1["so_full"])],
                      stats_gate=(f1["fin"], f1["full"]))
    f2 = emit_stats_finalize(s, bufs, "f1", wait=(p2["smm"], p2["smm_full"]))
    s.barrier()
    a1 = emit_apply(s, bufs, x1T, ACT, "a1", gate=[(p2["so"], p2["so_full"])])
    s.barrier()
    p4 = emit_m_path(s, bufs, Aall0, Bflat0, preps, ACT, mspill, "wo",
                     gate=[(a1["apd"], a1["full"]), ("cld", CLD_FULL)])
    for t in range(2):
        s.add("vector", lambda e, t=t: e.tensor_copy(nm2[t][:], bufs["pSxx"][t][:]),
              waits=[(p4["smm"], p4["smm_full"])], inc=("nmc", 1))
    s.barrier()
    p5 = emit_proj_spill(s, bufs, [{"w": WT0, "oc": 32, "bias": lambda j: bbs0[:, j:j + 1]}],
                         ACT, "wo", zspill, stats="norm",
                         gate=[(a1["apd"], a1["full"]), ("nmc", 2)], stats_gate=("nmc", 2))
    s.barrier()
    g0 = emit_gate(s, bufs, nm2, gam, "gt", wait=(p5["smm"], p5["smm_full"]))
    s.barrier()
    p6 = emit_combine(s, bufs, "c1", [(zspill, bufs["zf"]), (mspill, bufs["xs"])],
                      "gate", fspill, stats="meanvar",
                      gate=[(p5["so"], p5["so_full"]), (p4["so"], p4["so_full"])],
                      gam=gam, extra_vec_wait=(g0["g"], g0["full"]),
                      stats_gate=(g0["g"], g0["full"]))
    f3 = emit_stats_finalize(s, bufs, "f1", wait=(p6["smm"], p6["smm_full"]))
    s.barrier()
    p7 = emit_combine(s, bufs, "c1", [(fspill, bufs["zf"]), (x1T, bufs["xs"])],
                      "postln", x2T, stats="meanvar", gate=[(p6["so"], p6["so_full"])],
                      stats_gate=(f3["fin"], f3["full"]))
    f4 = emit_stats_finalize(s, bufs, "f1", wait=(p7["smm"], p7["smm_full"]))
    s.barrier()
    a2 = emit_apply(s, bufs, x2T, ACT, "a1", gate=[(p7["so"], p7["so_full"])])
    s.barrier()
    s.full_fence()
    specsB = [
        {"w": Wqkv[:, 0:4096], "oc": 32, "scale": QSCALE,
         "dst": lambda j: qspill1[128 * j:128 * (j + 1), :]},
        {"w": Wqkv[:, 4096:4224], "oc": 1, "scale": 1.0, "dst": lambda j: ccK1in[0:128, :]},
    ]
    pj1 = emit_proj_multi(s, bufs, specsB, ACT, tag="p",
                          act_ready=(a2["apd"], a2["full"]))
    s.full_fence()
    s.add("sync", lambda e: e.dma_start(out=wvst[0][:, 0:128], in_=Wqkv[0:128, 4224:4352]),
          waits=[(pj1["pe"], pj1["pe_full"]), (pj1["so"], pj1["so_full"])], inc=("v1s", 16))
    B_V1PM = s.c("vpm")
    B_V1SO = s.c("vso")
    for dc in range(NCH):
        if dc > 0:
            s.add("sync", lambda e, dc=dc: e.dma_start(
                out=wvst[dc % 2][:, 0:128], in_=Wqkv[128 * dc:128 * (dc + 1), 4224:4352]),
                waits=[("v1s", s.c("v1s"))], inc=("v1s", 16))
        s.add("gpsimd", lambda e, dc=dc: e.tensor_copy(bufs["sq"][dc % 2][:, 0:128],
                                                       wvst[dc % 2][:, 0:128]),
              waits=[("v1s", s.c("v1s"))], inc=("v1s", 1))
        for tc in range(8):
            wt = [("v1s", s.c("v1s"))]
            if dc == 0:
                wt += [(pj1["pe"], pj1["pe_full"]), (a2["apd"], a2["full"]),
                       ("vpe", s.c("vpe"))]
            s.add("tensor", lambda e, dc=dc, tc=tc: e.matmul(
                pbanks[tc][:, 0:128],
                ACT[:, TLOC * dc + 128 * tc: TLOC * dc + 128 * (tc + 1)],
                bufs["sq"][dc % 2][:, 0:128], start=(dc == 0), stop=(dc == NCH - 1)),
                waits=wt, inc=("v1s", 1))
    for tc in range(8):
        s.add("scalar", lambda e, tc=tc: e.activation(
            ost_v[tc % 2][:, 0:128], pbanks[tc][:, 0:128],
            AF.Copy), waits=[("v1s", s.c("v1s")), (pj1["so"], pj1["so_full"])],
            inc=("v1s", 1))
        s.add("gpsimd", lambda e, tc=tc: e.dma_start(
            out=ccV1in[128 * tc:128 * (tc + 1), :], in_=ost_v[tc % 2][:, 0:128]),
            waits=[("v1s", s.c("v1s"))], inc=("v1s", 16))

    s.full_fence()
    s.add("gpsimd", lambda e: e.collective_compute(
        "AllGather", ALU.bypass, replica_groups=[[0, 1, 2, 3], [4, 5, 6, 7]],
        ins=[ccK1in], outs=[ccK1out]),
        waits=[(pj1["so"], pj1["so_full"]), ("pz", PZ_FULL)], inc=("cc1", 1))
    s.add("gpsimd", lambda e: e.collective_compute(
        "AllGather", ALU.bypass, replica_groups=[[0, 1, 2, 3], [4, 5, 6, 7]],
        ins=[ccV1in], outs=[ccV1out]),
        waits=[("v1s", s.c("v1s")), ("vso", s.c("vso"))], inc=("cc1", 1))

    for rho in range(4):
        s.add("sync", lambda e, rho=rho: e.dma_start(
            out=bass.AP(KTg1.tensor, 384 + 128 * rho,
                        [[4480, 128], [512, 8], [1, 128]]),
            in_=bass.AP(ccK1out.tensor, (128 * rho) * 1024,
                        [[1024, 128], [128, 8], [1, 128]])),
            waits=[("cc1", 2), ("pz", PZ_FULL)], inc=("rr1", 16))
        s.add("sync", lambda e, rho=rho: e.dma_start(
            out=bass.AP(VLg1.tensor, (384 + 128 * rho) * 128,
                        [[65536, 8], [128, 128], [1, 128]]),
            in_=bass.AP(ccV1out.tensor, (1024 * rho) * 128,
                        [[16384, 8], [128, 128], [1, 128]])),
            waits=[("cc1", 2), ("pz", PZ_FULL)], inc=("rr1", 16))
    RR1_FULL = s.c("rr1")
    s.full_fence()
    at1 = emit_attn_fused(s, bufs, 1, qspill1, KTg1, VLg1, skFs, 20,
                          (corr1d, corr1s), slops, flag_sb, vsum_sb, ACT, "x",
                          gate=[("rr1", RR1_FULL), ("cld", CLD_FULL),
                                (pj1["pe"], pj1["pe_full"]), ("vpe", s.c("vpe")),
                                (pj1["so"], pj1["so_full"])])

    if debug:
        for nmk, srck, gatek in (("dbgK1", ccK1in, (pj1["so"], pj1["so_full"])),
                                 ("dbgV1", ccV1in, ("v1s", s.c("v1s")))):
            dap = nc.dram_tensor(nmk, list(srck.shape), BF16, kind="ExternalOutput").ap()
            s.add("gpsimd", lambda e, dap=dap, srck=srck: e.dma_start(out=dap, in_=srck),
                  waits=[gatek], inc=("dbg", 16))
        dbgA1 = nc.dram_tensor("dbgA1", [D, TLOC], BF16, kind="ExternalOutput").ap()
        for c in range(NCH):
            s.add("gpsimd", lambda e, c=c: e.dma_start(
                out=dbgA1[128 * c:128 * (c + 1), :], in_=ACT[:, TLOC * c:TLOC * (c + 1)]),
                waits=[(at1["nrm"], at1["nrm_full"])], inc=("dbg", 16))
    q1 = emit_proj_spill(s, bufs, [{"w": Wc, "oc": 32}], ACT, "wo", cspill,
                         stats="meanvar", gate=[(at1["nrm"], at1["nrm_full"])],
                         stats_gate=(f4["fin"], f4["full"]))
    g1f = emit_stats_finalize(s, bufs, "f1", wait=(q1["smm"], q1["smm_full"]))
    s.barrier()
    q2 = emit_combine(s, bufs, "c1", [(cspill, bufs["zf"]), (x2T, bufs["xs"])],
                      "postln", x3T, stats="meanvar", gate=[(q1["so"], q1["so_full"])],
                      stats_gate=(g1f["fin"], g1f["full"]))
    g2f = emit_stats_finalize(s, bufs, "f1", wait=(q2["smm"], q2["smm_full"]))
    s.barrier()
    a3 = emit_apply(s, bufs, x3T, ACT, "a1", gate=[(q2["so"], q2["so_full"])])
    s.barrier()
    pP, pSx = bufs["pP"], bufs["pP"][0:2] if False else (bufs["pP"], bufs["pSx"])
    pP, pSx = bufs["pP"], bufs["pSx"]
    wst, wbf = bufs["wst"], bufs["wbf"]
    s.add("sync", lambda e: e.dma_start(
        out=wst[0][:, 0:NCH * 8].rearrange("p (c o) -> p c o", c=NCH),
        in_=WrT.rearrange("(c p) o -> p c o", c=NCH)),
        waits=[(a3["apd"], a3["full"]), (q1["so"], q1["so_full"])], inc=("rlw", 16))
    s.add("gpsimd", lambda e: e.tensor_copy(wbf[0][:, 0:NCH * 8], wst[0][:, 0:NCH * 8]),
          waits=[("rlw", s.c("rlw"))], inc=("rcw", 1))
    s.add("sync", lambda e: e.dma_start(
        out=wst[0][:, 0:NCH * 128].rearrange("p (c o) -> p c o", c=NCH),
        in_=AallL.rearrange("(c p) o -> p c o", c=NCH)),
        waits=[("rcw", s.c("rcw"))], inc=("rlw", 16))
    s.add("gpsimd", lambda e: e.tensor_copy(wbf[1][:], wst[0][:, 0:NCH * 128]),
          waits=[("rlw", s.c("rlw"))], inc=("rcw", 1))
    RCW = s.c("rcw")
    for tb in range(2):
        for dc in range(NCH):
            s.add("tensor", lambda e, tb=tb, dc=dc: e.matmul(
                pP[tb][0:8, :], wbf[0][:, 8 * dc:8 * (dc + 1)],
                ACT[:, TLOC * dc + 512 * tb: TLOC * dc + 512 * (tb + 1)],
                start=(dc == 0), stop=(dc == NCH - 1)),
                waits=[("rcw", RCW - 1), (a3["apd"], a3["full"]),
                       (q1["so"], q1["so_full"])] if dc == 0 else [],
                inc=(("rpm", 1) if dc == NCH - 1 else None))
        s.add("vector", lambda e, tb=tb: e.tensor_scalar_add(
            lgf[:, 512 * tb:512 * (tb + 1)], pP[tb][0:8, :], brs[:, 0:1]),
            waits=[("rpm", s.c("rpm")), ("cld", CLD_FULL)], inc=("rle", 1))
        for dc in range(NCH):
            s.add("tensor", lambda e, tb=tb, dc=dc: e.matmul(
                pSx[tb][:], wbf[1][:, 128 * dc:128 * (dc + 1)],
                ACT[:, TLOC * dc + 512 * tb: TLOC * dc + 512 * (tb + 1)],
                start=(dc == 0), stop=(dc == NCH - 1)),
                waits=[("rcw", RCW), (g2f["fin"], g2f["full"])] if dc == 0 else [],
                inc=(("upm", 1) if dc == NCH - 1 else None))
        s.add("scalar", lambda e, tb=tb: e.activation(
            uf[:, 512 * tb:512 * (tb + 1)], pSx[tb][:], AF.Copy),
            waits=[("upm", s.c("upm"))], inc=("ue", 1))
    for tc in range(8):
        s.add("tensor", lambda e, tc=tc: e.transpose(
            pP[2][0:128, 0:8], lgf[0:8, 128 * tc:128 * (tc + 1)], id8s[:]),
            waits=[("rle", 2), ("cld", CLD_FULL)] if tc == 0 else [("lgev", s.c("lgev"))],
            inc=("lgtp", 1))
        s.add("vector", lambda e, tc=tc: e.tensor_copy(LGT[:, 8 * tc:8 * (tc + 1)], pP[2][0:128, 0:8]),
              waits=[("lgtp", s.c("lgtp"))], inc=("lgev", 1))
    CEs = [(0, 1), (2, 3), (4, 5), (6, 7), (0, 2), (1, 3), (4, 6), (5, 7), (1, 2), (5, 6),
           (0, 4), (3, 7), (1, 5), (2, 6), (1, 4), (3, 6), (2, 4), (3, 5), (3, 4)]
    s.add("vector", lambda e: e.tensor_copy(SRT[:], LGT[:]), waits=[("lgev", s.c("lgev"))],
          inc=("srt", 1))
    for (i, j) in CEs:
        vi = SRT[:].rearrange("p (g k) -> p k g", k=8)[:, i, :]
        vj = SRT[:].rearrange("p (g k) -> p k g", k=8)[:, j, :]
        tt = TAU[:].rearrange("p (g k) -> p k g", k=8)[:, 0, :]
        s.add("vector", lambda e, vi=vi, vj=vj, tt=tt: e.tensor_tensor(tt, vi, vj, op=ALU.min))
        s.add("vector", lambda e, vi=vi, vj=vj: e.tensor_max(vi, vi, vj))
        s.add("vector", lambda e, vj=vj, tt=tt: e.tensor_copy(vj, tt))
    for k in range(1, 8):
        vi = SRT[:].rearrange("p (g k) -> p k g", k=8)[:, k, :]
        vp = SRT[:].rearrange("p (g k) -> p k g", k=8)[:, k - 1, :]
        s.add("vector", lambda e, vi=vi, vp=vp: e.tensor_add(vi, vi, vp))
    for k in range(8):
        vi = SRT[:].rearrange("p (g k) -> p k g", k=8)[:, k, :]
        ti = TAU[:].rearrange("p (g k) -> p k g", k=8)[:, k, :]
        s.add("vector", lambda e, vi=vi, ti=ti, k=k: e.tensor_scalar(
            ti, vi, -1.0, 1.0 / (k + 1), op0=ALU.add, op1=ALU.mult),
            inc=(("tau", 1) if k == 7 else None))
    for g in range(8):
        s.add("vector", lambda e, g=g: e.reduce_max(
            tauc[:, g:g + 1], TAU[:, 8 * g:8 * (g + 1)], axis=AX.X),
            waits=[("tau", 1)], inc=("taur", 1))
    for g in range(8):
        s.add("vector", lambda e, g=g: e.tensor_scalar(
            WTOK[:, 8 * g:8 * (g + 1)], LGT[:, 8 * g:8 * (g + 1)], tauc[:, g:g + 1], 0.0,
            op0=ALU.subtract, op1=ALU.max), waits=[("taur", s.c("taur"))],
            inc=(("wtok", 1) if g == 7 else None))
    for tc in range(8):
        dbi = tc % 2
        wcp = [("wtok", 1)]
        if tc >= 2:
            wcp.append(("wttp", s.c("wttp") - 1))
        for r in range(16):
            s.add("vector", lambda e, tc=tc, r=r, dbi=dbi: e.tensor_copy(
                W16tok[dbi][:].rearrange("p (e r) -> p r e", r=16)[:, r, :],
                WTOK[:, 8 * tc:8 * (tc + 1)]),
                waits=wcp if r == 0 else [], inc=(("w16c", 1) if r == 15 else None))
        s.add("tensor", lambda e, tc=tc, dbi=dbi: e.transpose(
            pP[3][:, 0:128], W16tok[dbi][:], id128s[:]),
            waits=[("w16c", s.c("w16c")), ("cld", CLD_FULL)] + ([("repe", s.c("repe"))] if tc >= 1 else []),
            inc=("wttp", 1))
        s.add("vector", lambda e, tc=tc: e.tensor_copy(
            W16T[:, 128 * tc:128 * (tc + 1)], pP[3][:, 0:128]),
            waits=[("wttp", s.c("wttp"))], inc=("repe", 1))
    for tb in range(2):
        s.add("vector", lambda e, tb=tb: e.tensor_mul(
            ub2[:, 512 * tb:512 * (tb + 1)], uf[:, 512 * tb:512 * (tb + 1)],
            W16T[:, 512 * tb:512 * (tb + 1)]),
            waits=[("repe", 8), ("ue", 2)], inc=("ub2d", 1))
    s.barrier()
    s.add("sync", lambda e: e.dma_start(out=wst[0][:, 0:4096], in_=BflatL),
          waits=[("ub2d", 2), ("rcw", s.c("rcw"))], inc=("rlw", 16))
    s.add("gpsimd", lambda e: e.tensor_copy(wbf[1][:], wst[0][:, 0:4096]),
          waits=[("rlw", s.c("rlw"))], inc=("rcw", 1))
    zf = bufs["zf"]
    BCW = s.c("rcw")
    B_BPM, B_BPE, B_BSO = s.c("wopm"), s.c("wope"), s.c("woso")
    for ow in range(NCH):
        for tb in range(2):
            bank = (ow % 2) * 2 + tb
            wt = [("rcw", BCW), ("ub2d", 2), ("rle", 2), ("lgev", s.c("lgev")),
                  ("repe", s.c("repe"))]
            if ow >= 2:
                wt.append(("wope", B_BPE + 2 * (ow - 2) + tb + 1))
            s.add("tensor", lambda e, ow=ow, tb=tb, bank=bank: e.matmul(
                pP[bank][:], wbf[1][:, 128 * ow:128 * (ow + 1)],
                ub2[:, 512 * tb:512 * (tb + 1)], start=True, stop=True),
                waits=wt, inc=("wopm", 1))
            ev_w = [("wopm", s.c("wopm"))]
            if ow >= 2 and tb == 0:
                ev_w.append(("woso", B_BSO + 16 * (ow - 1)))
            s.add("scalar", lambda e, ow=ow, tb=tb, bank=bank: e.activation(
                zf[ow % 2][:, 512 * tb:512 * (tb + 1)], pP[bank][:], AF.Copy),
                waits=ev_w, inc=("wope", 1))
        s.add("gpsimd", lambda e, ow=ow: e.dma_start(
            out=eospill[128 * ow:128 * (ow + 1), :], in_=zf[ow % 2][:]),
            waits=[("wope", s.c("wope"))], inc=("woso", 16))
    s.barrier()
    q6 = emit_proj_spill(s, bufs, [{"w": WTb, "oc": 32, "bias": lambda j: bbsE[:, j:j + 1]}],
                         ACT, "wo", basespill, stats="none",
                         gate=[(a3["apd"], a3["full"]), ("cld", CLD_FULL), ("wope", s.c("wope"))])
    s.barrier()
    q7 = emit_combine(s, bufs, "c1", [(basespill, bufs["zf"]), (eospill, bufs["xs"])],
                      "add", f1spill, stats="meanvar",
                      gate=[(q6["so"], q6["so_full"]), ("woso", s.c("woso"))],
                      stats_gate=("ub2d", 2))
    g3f = emit_stats_finalize(s, bufs, "f1", wait=(q7["smm"], q7["smm_full"]))
    s.barrier()
    q8 = emit_combine(s, bufs, "c1", [(f1spill, bufs["zf"]), (x3T, bufs["xs"])],
                      "postln", xoutT, stats="none", gate=[(q7["so"], q7["so_full"])])
    s.add("gpsimd", lambda e: e.nop(), waits=[(q8["so"], q8["so_full"])])

    s.realize(nc, es, block)
    es.close()
    lower_extended_insts(nc)
    return nc


def kernel_fused(L0_attn_x, L0_attn_Wq, L0_attn_Wk, L0_attn_Wv, L0_attn_Wo,
                 L0_ffn_W, L0_ffn_b, L0_ffn_A, L0_ffn_B, L0_ffn_p_scores,
                 L1_attn_w_qkv, L1_attn_w_c,
                 L1_ffn_W_base, L1_ffn_b_base, L1_ffn_W_router, L1_ffn_b_router,
                 L1_ffn_lora_A, L1_ffn_lora_B):
    f32 = np.float32
    asf = lambda a: np.ascontiguousarray(np.asarray(a, dtype=f32))
    x0 = asf(L0_attn_x)
    xT_shards = shard_xT(x0)
    nc = build_fused()
    slopes = np.array(ALIBI_SLOPES, dtype=f32)
    shared = {
        "Wq": asf(L0_attn_Wq), "Wk": asf(L0_attn_Wk), "Wv": asf(L0_attn_Wv),
        "Wo": asf(L0_attn_Wo),
        "WT0": asf(np.asarray(L0_ffn_W).T),
        "bb0": asf(np.asarray(L0_ffn_b).reshape(32, 128).T),
        "Aall0": asf(np.asarray(L0_ffn_A).transpose(1, 0, 2).reshape(4096, 128)),
        "Bflat0": asf(np.asarray(L0_ffn_B).reshape(128, 4096)),
        "prep0": asf(np.repeat(np.asarray(L0_ffn_p_scores), 16).reshape(128, 1)),
        "Wqkv": asf(L1_attn_w_qkv), "Wc": asf(L1_attn_w_c),
        "WTb": asf(np.asarray(L1_ffn_W_base).T),
        "bbb": asf(np.asarray(L1_ffn_b_base).reshape(32, 128).T),
        "WrT": asf(np.asarray(L1_ffn_W_router).T),
        "brt": asf(np.asarray(L1_ffn_b_router).reshape(8, 1)),
        "AallL": asf(np.asarray(L1_ffn_lora_A).transpose(1, 0, 2).reshape(4096, 128)),
        "BflatL": asf(np.asarray(L1_ffn_lora_B).reshape(128, 4096)),
        "id8": np.eye(8, dtype=f32), "id128": np.eye(128, dtype=f32),
        "slopA": np.ascontiguousarray(np.tile(slopes, (128, 1))),
    }
    in_maps = []
    for core in range(NCORE):
        skF, corr0, corr1, flag = _sched_tables(core)
        im = dict(shared)
        im["x0T"] = xT_shards[core]
        im["skFd"] = skF
        im["corr0d"] = corr0
        im["corr1d"] = corr1
        im["flagd"] = flag
        in_maps.append(im)
    res = run_bass_kernel_spmd(nc, in_maps, core_ids=list(range(NCORE)))
    return unshard_T([res.results[c]["xoutT"] for c in range(NCORE)], dtype=f32)


def kernel(**inputs):
    return kernel_fused(**inputs)
